# version 14
# baseline (speedup 1.0000x reference)
import numpy as np
import ml_dtypes
import concourse.bass as bass
import concourse.mybir as mybir
from concourse.bass_utils import run_bass_kernel_spmd
from contextlib import ExitStack

F32 = mybir.dt.float32
BF16 = mybir.dt.bfloat16
AF = mybir.ActivationFunctionType
OP = mybir.AluOpType
AX = mybir.AxisListType.X

NCORES = 8
D = 1024
QKVD = 3072
DH = 4096
T = 256
NSUB = 2
NTILES = 16
ENGS = ("sync", "gpsimd", "tensor", "vector", "scalar")


class Em:
    def __init__(self):
        self.ops = {e: [] for e in ENGS}
        self.cnt = {}
        self.ctx = ""

    def wait(self, eng, sem, val):
        if val > 0:
            self.ops[eng].append(("w", sem, val, self.ctx))

    def op(self, eng, fn, inc=None, k=1):
        if inc is not None:
            self.cnt[inc] = self.cnt.get(inc, 0) + k
        self.ops[eng].append(("o", fn, inc, k, self.ctx))
        return self.cnt.get(inc, 0)

    def val(self, sem):
        return self.cnt.get(sem, 0)


class Pool:
    def __init__(self, em, items):
        self.em = em
        self.items = list(items)
        self.free_at = {i: None for i in items}
        self.rr = 0

    def acquire(self, eng):
        it = self.items[self.rr % len(self.items)]
        self.rr += 1
        fa = self.free_at[it]
        if fa is not None:
            self.em.wait(eng, fa[0], fa[1])
        return it

    def release(self, it, sem, val):
        self.free_at[it] = (sem, val)


def build_nc(ntiles=NTILES):
    nc = bass.Bass("TRN2", target_bir_lowering=False, debug=False)
    hx = nc.dram_tensor("x", [ntiles, NSUB, 128, D], F32, kind="ExternalInput").ap()
    ho = nc.dram_tensor("out", [ntiles, NSUB, 128, D], F32, kind="ExternalOutput").ap()
    hwq = nc.dram_tensor("wq", [12, 128, 2048], F32, kind="ExternalInput").ap()
    hwo = nc.dram_tensor("wo", [4, 128, 2048], F32, kind="ExternalInput").ap()
    hw1 = nc.dram_tensor("w1", [8, 128, 4096], F32, kind="ExternalInput").ap()
    hw2 = nc.dram_tensor("w2", [8, 128, 4096], F32, kind="ExternalInput").ap()
    hh1 = nc.dram_tensor("h1", [32, 128, 4096], F32, kind="ExternalInput").ap()
    hh2 = nc.dram_tensor("h2", [32, 128, 4096], F32, kind="ExternalInput").ap()
    hw3 = nc.dram_tensor("w3", [8, 128, 4096], F32, kind="ExternalInput").ap()
    hbq = nc.dram_tensor("bqbc", [128, QKVD], F32, kind="ExternalInput").ap()
    hbo = nc.dram_tensor("bobc", [128, D], F32, kind="ExternalInput").ap()
    hs1 = nc.dram_tensor("s1", [128, 32], F32, kind="ExternalInput").ap()
    hs2 = nc.dram_tensor("s2", [128, 32], F32, kind="ExternalInput").ap()
    hh1s = nc.dram_tensor("h1s", [128, 32], F32, kind="ExternalInput").ap()
    hh2s = nc.dram_tensor("h2s", [128, 32], F32, kind="ExternalInput").ap()
    hs3 = nc.dram_tensor("s3bc", [128, D], F32, kind="ExternalInput").ap()
    hidf = nc.dram_tensor("idf", [128, 128], F32, kind="ExternalInput").ap()
    hidb = nc.dram_tensor("idb", [128, 128], BF16, kind="ExternalInput").ap()
    zw1 = nc.dram_tensor("zw1", [8, 128, 4096], BF16, kind="Internal").ap()
    zw2 = nc.dram_tensor("zw2", [8, 128, 4096], BF16, kind="Internal").ap()
    zh1 = nc.dram_tensor("zh1", [32, 128, 4096], BF16, kind="Internal").ap()
    zh2 = nc.dram_tensor("zh2", [32, 128, 4096], BF16, kind="Internal").ap()
    zw3 = nc.dram_tensor("zw3", [8, 128, 4096], BF16, kind="Internal").ap()
    zwq = nc.dram_tensor("zwq", [12, 128, 4096], BF16, kind="Internal").ap()
    zwo = nc.dram_tensor("zwo", [4, 128, 4096], BF16, kind="Internal").ap()

    with ExitStack() as st:
        def sb(name, shape, dt=F32):
            return st.enter_context(nc.sbuf_tensor(name, shape, dt))

        regA = sb("regA", [128, 8192])
        regB = sb("regB", [128, 8192])
        qkvt = sb("qkvt", [128, 6144])
        patt = sb("patt", [128, 2048])
        ctxa = sb("ctxa", [128, 3392])
        x_t = [sb("x_t0", [128, 2048]), sb("x_t1", [128, 2048])]
        xfm = sb("xfm", [128, 2048])
        ring = sb("ring", [128, 8192])
        ln1 = sb("ln1", [128, 2048])
        l1f = sb("l1f", [128, 2048])
        fmst = sb("fmst", [128, 2048])
        lnb = sb("lnb", [128, 64])
        s1_sb = sb("s1_sb", [128, 32])
        s2_sb = sb("s2_sb", [128, 32])
        h1s_sb = sb("h1s_sb", [128, 32])
        h2s_sb = sb("h2s_sb", [128, 32])
        s3_sb = sb("s3_sb", [128, D])
        idf_sb = sb("idf_sb", [128, 128])
        idb_sb = sb("idb_sb", [128, 128], BF16)
        p = [st.enter_context(nc.psum_tensor(f"p{i}", [128, 512], F32))
             for i in range(8)]

        sem_names = ["cst", "x", "st", "mm", "v", "sc", "bs", "sw", "gp",
                     "sl0", "sl1", "sl2", "sl3"]
        sems = {n: st.enter_context(nc.semaphore(n)) for n in sem_names}
        blk = st.enter_context(nc.Block())

        A = regA[:]
        B = regB[:]
        Ab = A.bitcast(BF16)
        Bb = B.bitcast(BF16)
        pb = patt[:].bitcast(BF16)

        h_hi = Ab[:, 0:8192]
        h_lo = Ab[:, 8192:16384]
        g1_hi = Bb[:, 0:8192]
        g1_lo = Bb[:, 8192:16384]
        out_tm = B[:, 0:2048]
        sq2 = B[:, 2048:4096]
        y2 = B[:, 4096:6144]
        bobc = B[:, 2048:3072]
        bqbc = B[:, 3072:6144]
        Pscr = patt[:]
        hi_tm = pb[:, 0:2048]
        lo_tm = pb[:, 2048:4096]
        ctx_tm = ctxa[:, 0:2048]
        S_s = [ctxa[:, 2048:2304], ctxa[:, 2304:2560]]
        E_s = [ctxa[:, 2560:2816], ctxa[:, 2816:3072]]
        A_b = ctxa[:, 3072:3328]
        M_s = [ctxa[:, 3328:3344], ctxa[:, 3344:3360]]
        R_b = ctxa[:, 3360:3376]
        Rr_b = ctxa[:, 3376:3392]
        l1_hi = l1f[:].bitcast(BF16)[:, 0:2048]
        l1_lo = l1f[:].bitcast(BF16)[:, 2048:4096]
        xf_hi = xfm[:].bitcast(BF16)[:, 0:2048]
        xf_lo = xfm[:].bitcast(BF16)[:, 2048:4096]


        slots = []
        for i in range(4):
            f32v = ring[:, i * 2048:(i + 1) * 2048]
            bfv = ring[:].bitcast(BF16)[:, i * 4096:(i + 1) * 4096]
            slots.append({"f32": f32v, "bf": bfv, "sem": f"sl{i}", "n": 0,
                          "idx": i})

        em = Em()

        def vpad(n=8):
            for _ in range(n):
                em.op("vector", lambda v: v.engine_nop())

        pa_pool = Pool(em, [0, 1, 2, 3, 4, 5])
        pt_pool = Pool(em, [6, 7])
        fm_pool = Pool(em, list(range(8)))
        slot_free = {i: [] for i in range(4)}

        def slot_dma(slot, src_ap, dst_bf):
            i = slot["idx"]
            for w_ in slot_free[i]:
                em.wait("gpsimd", *w_)
            slot_free[i] = []
            dst = slot["bf"] if dst_bf else slot["f32"]
            em.op("gpsimd", lambda g, d=dst, s=src_ap: g.dma_start(out=d, in_=s),
                  inc=slot["sem"], k=16)
            slot["n"] += 1
            return 16 * slot["n"]

        def slot_done(slot, sem, val):
            slot_free[slot["idx"]].append((sem, val))

        def scratch_write(slot, dst_ap, wv_):
            em.wait("scalar", slot["sem"], wv_)
            swv = em.op("scalar", lambda sc, d=dst_ap, s=slot["bf"]:
                        sc.dma_start(out=d, in_=s), inc="sw", k=16)
            slot_done(slot, "sw", swv)

        stg_free = {2: 0, 3: 0}

        def split_write(slot, wv_, dst_ap, si):
            stg = slots[2 + (si & 1)]
            em.wait("scalar", slot["sem"], wv_)
            if stg_free[stg["idx"]]:
                em.wait("scalar", "sw", stg_free[stg["idx"]])
            hv = em.op("scalar", lambda sc, d=stg["bf"], s=slot["f32"]:
                       sc.copy(d[:, 0:2048], s), inc="sc")
            em.wait("vector", "sc", hv)
            lv = em.op("vector", lambda v, d=stg["bf"], s=slot["f32"]:
                       v.tensor_sub(d[:, 2048:4096], s, d[:, 0:2048]), inc="v")
            em.wait("scalar", "v", lv)
            swv = em.op("scalar", lambda sc, d=dst_ap, s=stg["bf"]:
                        sc.dma_start(out=d, in_=s), inc="sw", k=16)
            stg_free[stg["idx"]] = swv
            slot_free[stg["idx"]].append(("sw", swv))
            slot_free[slot["idx"]].append(("v", lv))

        consts = [(s1_sb[:], hs1), (s2_sb[:], hs2),
                  (h1s_sb[:], hh1s), (h2s_sb[:], hh2s), (s3_sb[:], hs3),
                  (idf_sb[:], hidf), (idb_sb[:], hidb)]
        for d_, s_ in consts:
            em.op("sync", lambda sy, d=d_, s=s_: sy.dma_start(out=d, in_=s),
                  inc="cst", k=16)
        cst_all = em.val("cst")
        for t_ in range(min(2, ntiles)):
            for s_ in range(NSUB):
                em.op("sync", lambda sy, t=t_, s=s_: sy.dma_start(
                    out=x_t[t][:, s * D:(s + 1) * D], in_=hx[t, s]),
                    inc="x", k=16)

        ST = {
            "ln1_apply": [0, 0],
            "outproj_mm": 0,
            "ln1tp_mm": 0,
            "out_done": 0,
            "ctx_done": {},
            "l1f_done": {},
            "split_done": {},
        }

        xv2 = [x_t[0][:].rearrange("q (s f) -> q s f", f=D),
               x_t[1][:].rearrange("q (s f) -> q s f", f=D)]
        qkvv = qkvt[:].rearrange("q (s f) -> q s f", f=QKVD)

        def emit_head(ti):
            em.ctx = f"head{ti}"
            xb = ti % 2
            if ti == 0:
                em.wait("tensor", "cst", cst_all)
                em.wait("vector", "cst", cst_all)
                em.wait("scalar", "cst", cst_all)
            if ST["out_done"]:
                em.wait("sync", "v", ST["out_done"])
            em.wait("sync", "mm", em.val("mm"))
            em.op("sync", lambda sy: sy.dma_start(out=bobc, in_=hbo),
                  inc="bs", k=16)
            em.op("sync", lambda sy: sy.dma_start(out=bqbc, in_=hbq),
                  inc="bs", k=16)
            bias_v = em.val("bs")
            em.wait("tensor", "x", 16 * NSUB * (ti + 1))
            xe = 0
            if ti == 0:
                for grp in range(4):
                    s_i, cc = grp // 2, grp % 2
                    bank = pt_pool.acquire("tensor")
                    for q in range(4):
                        c = cc * 4 + q
                        fn = (lambda t, b=bank, q=q, s=s_i, c=c, xb=xb:
                              t.transpose(p[b][:, q * 128:(q + 1) * 128],
                                          x_t[xb][:, s * D + c * 128:s * D + (c + 1) * 128],
                                          idf_sb[:]))
                        mmv = em.op("tensor", fn, inc=("mm" if q == 3 else None))
                    em.wait("vector", "mm", mmv)
                    dst = xfm[:].rearrange("q (c t) -> q c t", t=256)[
                        :, cc * 4:cc * 4 + 4, s_i * 128:(s_i + 1) * 128]
                    src = p[bank][:, 0:512].rearrange("q (c t) -> q c t", t=128)
                    vv = em.op("vector", lambda v, d=dst, s=src:
                               v.tensor_copy(d, s), inc="v")
                    pt_pool.release(bank, "v", vv)
                    xe = vv
            else:
                if ST["ln1tp_mm"]:
                    em.wait("scalar", "mm", ST["ln1tp_mm"])
                hv = em.op("scalar", lambda sc, xb=xb:
                           sc.copy(hi_tm, x_t[xb][:]), inc="sc")
                em.wait("vector", "sc", hv)
                lv = em.op("vector", lambda v, xb=xb: v.tensor_sub(
                    lo_tm, x_t[xb][:], hi_tm), inc="v")
                for buf_i, (src_tm, dst_fm) in enumerate(
                        [(hi_tm, xf_hi), (lo_tm, xf_lo)]):
                    for cc in range(2):
                        bank = pt_pool.acquire("tensor")
                        if buf_i == 0 and cc == 0:
                            em.wait("tensor", "sc", hv)
                        if buf_i == 1 and cc == 0:
                            em.wait("tensor", "v", lv)
                        for q in range(4):
                            c = cc * 4 + q
                            for s_i in range(NSUB):
                                fn = (lambda t, b=bank, q=q, s=s_i, c=c,
                                      src=src_tm: t.transpose(
                                          p[b][:].bitcast(BF16)[
                                              :, (q * 2 + s) * 128:(q * 2 + s + 1) * 128],
                                          src.rearrange("q (s f) -> q s f", f=D)[
                                              :, s, c * 128:(c + 1) * 128],
                                          idb_sb[:]))
                                mmv = em.op("tensor", fn,
                                            inc=("mm" if (q == 3 and s_i == 1)
                                                 else None))
                        em.wait("vector", "mm", mmv)
                        if buf_i == 0 and cc == 0 and ST["outproj_mm"]:
                            em.wait("vector", "mm", ST["outproj_mm"])
                        dst = dst_fm.rearrange("q (c t) -> q c t", t=256)[
                            :, cc * 4:cc * 4 + 4, :]
                        src = p[bank][:].bitcast(BF16)[:, 0:1024] \
                            .rearrange("q (c t) -> q c t", t=256)
                        vv = em.op("vector", lambda v, d=dst, s=src:
                                   v.tensor_copy(d, s), inc="v")
                        pt_pool.release(bank, "v", vv)
                        xe = vv
            ST["xtp_mm"] = mmv
            em.wait("vector", "bs", bias_v)
            for s_i in range(NSUB):
                fn_ = (lambda v, s=s_i, xb=xb: v.tensor_tensor(
                    x_t[xb][:, s * D:(s + 1) * D],
                    x_t[xb][:, s * D:(s + 1) * D], bobc[:], OP.add))
                em.op("vector", fn_, inc="v")
            for g in range(12):
                slot = slots[g % 2]
                if ti == 0:
                    wv_ = slot_dma(slot, hwq[g], dst_bf=False)
                    split_write(slot, wv_, zwq[g], g)
                else:
                    wv_ = slot_dma(slot, zwq[g], dst_bf=True)
                last_mm = 0
                for s_i in range(NSUB):
                    bank = pa_pool.acquire("tensor")
                    if s_i == 0:
                        em.wait("tensor", slot["sem"], wv_)
                        em.wait("tensor", "v", xe)
                    if ti == 0:
                        for k in range(8):
                            fn = (lambda t, b=bank, k=k, s=s_i, sl=slot:
                                  t.matmul(p[b][:, 0:256],
                                           xfm[:, k * 256 + s * 128:k * 256 + (s + 1) * 128],
                                           sl["f32"][:, k * 256:(k + 1) * 256],
                                           start=(k == 0), stop=(k == 7)))
                            mmv = em.op("tensor", fn,
                                        inc=("mm" if k == 7 else None))
                    else:
                        passes = [(xf_hi, 0), (xf_hi, 2048), (xf_lo, 0)]
                        for pi, (xbuf, roff) in enumerate(passes):
                            for k in range(8):
                                fn = (lambda t, b=bank, k=k, s=s_i, sl=slot,
                                      xbuf=xbuf, roff=roff, pi=pi: t.matmul(
                                          p[b][:, 0:256],
                                          xbuf[:, k * 256 + s * 128:k * 256 + (s + 1) * 128],
                                          sl["bf"][:, roff + k * 256:roff + (k + 1) * 256],
                                          start=(pi == 0 and k == 0),
                                          stop=(pi == 2 and k == 7)))
                                mmv = em.op("tensor", fn,
                                            inc=("mm" if (pi == 2 and k == 7)
                                                 else None))
                    last_mm = mmv
                    em.wait("vector", "mm", mmv)
                    dst = qkvv[:, s_i, g * 256:(g + 1) * 256]
                    vv = em.op("vector", lambda v, d=dst, b=bank, g=g:
                               v.tensor_tensor(d, p[b][:, 0:256],
                                               bqbc[:, g * 256:(g + 1) * 256],
                                               OP.add), inc="v")
                    pa_pool.release(bank, "v", vv)
                slot_done(slot, "mm", last_mm)

        def emit_midA(ti, s_i):
            em.ctx = f"mid{ti}"
            if s_i == 0 and ST.get("xtp_mm"):
                em.wait("vector", "mm", ST["xtp_mm"])
            Pv = Pscr.rearrange("q (h g d) -> q h g d", g=16, d=64)
            S_b, M_b = S_s[s_i], M_s[s_i]
            qk = qkvv[:, s_i, :].rearrange("q (h j) -> q h j", j=192)
            for hf in range(8):
                qa = qk[:, 2 * hf:2 * (hf + 1), 0:64].unsqueeze(2) \
                    .broadcast_to((128, 2, 16, 64))
                ka = qk[:, 0:16, 64:128].unsqueeze(1) \
                    .broadcast_to((128, 2, 16, 64))
                em.op("vector", lambda v, qa=qa, ka=ka:
                      v.tensor_tensor(Pv[:, 0:2], qa, ka, OP.mult))
                sdst = S_b[:, 32 * hf:32 * (hf + 1)] \
                    .rearrange("q (h g) -> q h g", g=16)
                em.op("vector", lambda v, d=sdst:
                      v.reduce_sum(d, Pv[:, 0:2], axis=AX))
            Sv = S_b.rearrange("q (h g) -> q h g", g=16)
            em.op("vector", lambda v, S=Sv, M=M_b: v.reduce_max(M, S, axis=AX))
            vpad(4)
            mbc = M_b.unsqueeze(2).broadcast_to((128, 16, 16))
            zv = em.op("vector", lambda v, S=Sv, m=mbc:
                       v.tensor_tensor(S, S, m, OP.subtract), inc="v")
            ST[f"zv{s_i}"] = zv

        def emit_midB(ti):
            em.ctx = f"mid{ti}"
            for s_i in range(NSUB):
                em.wait("scalar", "v", ST[f"zv{s_i}"])
                ev = em.op("scalar", lambda sc, s=s_i: sc.activation(
                    E_s[s], S_s[s], AF.Exp, scale=0.125), inc="sc")
                ST[f"ev{s_i}"] = ev

        def emit_midC(ti, s_i):
            em.ctx = f"mid{ti}"
            P2v = Pscr.rearrange("q (h d g) -> q h d g", d=64, g=16)
            E_b = E_s[s_i]
            qk = qkvv[:, s_i, :].rearrange("q (h j) -> q h j", j=192)
            em.wait("vector", "sc", ST[f"ev{s_i}"])
            Ev = E_b.rearrange("q (h g) -> q h g", g=16)
            em.op("vector", lambda v, E=Ev: v.reduce_sum(R_b, E, axis=AX))
            vpad(6)
            em.op("vector", lambda v: v.reciprocal(Rr_b, R_b))
            vpad(6)
            rbc = Rr_b.unsqueeze(2).broadcast_to((128, 16, 16))
            em.op("vector", lambda v, r=rbc, E=Ev: v.tensor_tensor(
                A_b.rearrange("q (h g) -> q h g", g=16), E, r, OP.mult))
            last = 0
            for hf in range(8):
                Aex = A_b.rearrange("q (h g) -> q h g", g=16)[
                    :, 2 * hf:2 * (hf + 1), :].unsqueeze(2) \
                    .broadcast_to((128, 2, 64, 16))
                Vex = qk[:, 0:16, 128:192].transpose([0, 2, 1]) \
                    .unsqueeze(1).broadcast_to((128, 2, 64, 16))
                em.op("vector", lambda v, a=Aex, w=Vex:
                      v.tensor_tensor(P2v[:, 0:2], a, w, OP.mult))
                cdst = ctx_tm.rearrange("q (s f) -> q s f", f=D)[
                    :, s_i, 128 * hf:128 * (hf + 1)] \
                    .rearrange("q (h d) -> q h d", d=64)
                last = em.op("vector", lambda v, d=cdst:
                             v.reduce_sum(d, P2v[:, 0:2], axis=AX), inc="v")
            if s_i == NSUB - 1:
                ST["ctx_done"][ti] = last

        def emit_mid(ti):
            emit_midA(ti, 0)
            emit_midA(ti, 1)
            emit_midB(ti)
            emit_midC(ti, 0)
            emit_midC(ti, 1)

        def emit_tail(ti):
            em.ctx = f"tail{ti}"
            xb = ti % 2
            ctxfm_done = 0
            if ti == 0:
                for s_i in range(NSUB):
                    for cc in range(2):
                        bank = pt_pool.acquire("tensor")
                        em.wait("tensor", "v", ST["ctx_done"][ti])
                        for q in range(4):
                            c = cc * 4 + q
                            fn = (lambda t, b=bank, q=q, s=s_i, c=c: t.transpose(
                                p[b][:, q * 128:(q + 1) * 128],
                                ctx_tm.rearrange("q (s f) -> q s f", f=D)[
                                    :, s, c * 128:(c + 1) * 128],
                                idf_sb[:]))
                            mmv = em.op("tensor", fn,
                                        inc=("mm" if q == 3 else None))
                        em.wait("vector", "mm", mmv)
                        dst = xfm[:].rearrange("q (c t) -> q c t", t=256)[
                            :, cc * 4:cc * 4 + 4, s_i * 128:(s_i + 1) * 128]
                        src = p[bank][:, 0:512].rearrange("q (c t) -> q c t",
                                                          t=128)
                        vv = em.op("vector", lambda v, d=dst, s=src:
                                   v.tensor_copy(d, s), inc="v")
                        pt_pool.release(bank, "v", vv)
                        ctxfm_done = vv
            else:
                ctxfm_done = ST["ctxfm_done"]
            if ti == 0:
                for g in range(4):
                    slot = slots[g % 2]
                    wv_ = slot_dma(slot, hwo[g], dst_bf=False)
                    split_write(slot, wv_, zwo[g], g)
                    last_mm = 0
                    for s_i in range(NSUB):
                        bank = pa_pool.acquire("tensor")
                        if s_i == 0:
                            em.wait("tensor", slot["sem"], wv_)
                            em.wait("tensor", "v", ctxfm_done)
                        for k in range(8):
                            fn = (lambda t, b=bank, k=k, s=s_i, sl=slot:
                                  t.matmul(p[b][:, 0:256],
                                           xfm[:, k * 256 + s * 128:k * 256 + (s + 1) * 128],
                                           sl["f32"][:, k * 256:(k + 1) * 256],
                                           start=(k == 0), stop=(k == 7)))
                            mmv = em.op("tensor", fn,
                                        inc=("mm" if k == 7 else None))
                        last_mm = mmv
                        em.wait("vector", "mm", mmv)
                        dst = xv2[xb][:, s_i, g * 256:(g + 1) * 256]
                        vv = em.op("vector", lambda v, d=dst, b=bank:
                                   v.tensor_tensor(d, p[b][:, 0:256], d,
                                                   OP.add), inc="v")
                        pa_pool.release(bank, "v", vv)
                    slot_done(slot, "mm", last_mm)
                resid_s = [0, 0]
                resid_s[NSUB - 1] = vv
                resid_done = vv
            else:
                wo_wv = {}
                for g in range(4):
                    wo_wv[g] = slot_dma(slots[g], zwo[g], dst_bf=True)
                resid_s = [0, 0]
                last_mm = 0
                for s_i in range(NSUB):
                    for g in range(4):
                        slot = slots[g]
                        bank = pa_pool.acquire("tensor")
                        if s_i == 0:
                            em.wait("tensor", slot["sem"], wo_wv[g])
                            if g == 0:
                                em.wait("tensor", "v", ctxfm_done)
                        passes = [(xf_hi, 0), (xf_hi, 2048), (xf_lo, 0)]
                        for pi, (xbuf, roff) in enumerate(passes):
                            for k in range(8):
                                fn = (lambda t, b=bank, k=k, s=s_i, sl=slot,
                                      xbuf=xbuf, roff=roff, pi=pi: t.matmul(
                                          p[b][:, 0:256],
                                          xbuf[:, k * 256 + s * 128:k * 256 + (s + 1) * 128],
                                          sl["bf"][:, roff + k * 256:roff + (k + 1) * 256],
                                          start=(pi == 0 and k == 0),
                                          stop=(pi == 2 and k == 7)))
                                mmv = em.op("tensor", fn,
                                            inc=("mm" if (pi == 2 and k == 7)
                                                 else None))
                        last_mm = mmv
                        if s_i == NSUB - 1:
                            slot_done(slot, "mm", mmv)
                        em.wait("vector", "mm", mmv)
                        dst = xv2[xb][:, s_i, g * 256:(g + 1) * 256]
                        vv = em.op("vector", lambda v, d=dst, b=bank:
                                   v.tensor_tensor(d, p[b][:, 0:256], d,
                                                   OP.add), inc="v")
                        pa_pool.release(bank, "v", vv)
                    resid_s[s_i] = vv
                    em.op("vector", lambda v, xb=xb, s=s_i: v.reduce_sum(
                        lnb[:, s:s + 1], xv2[xb][:, s:s + 1, :], axis=AX))
                    em.wait("scalar", "v", vv)
                    scv0 = em.op("scalar", lambda sc, xb=xb, s=s_i:
                                 sc.activation(
                                     Pscr[:, s * D:(s + 1) * D],
                                     x_t[xb][:, s * D:(s + 1) * D],
                                     AF.Square), inc="sc")
                    em.wait("vector", "sc", scv0)
                    em.op("vector", lambda v, s=s_i: v.reduce_sum(
                        lnb[:, 4 + s:5 + s],
                        Pscr[:, s * D:(s + 1) * D].unsqueeze(1), axis=AX))
                resid_done = vv
            ST["outproj_mm"] = last_mm
            if ti == 0:
                em.op("vector", lambda v, xb=xb: v.reduce_sum(
                    lnb[:, 0:2], xv2[xb], axis=AX))
                em.wait("scalar", "v", resid_done)
                scv = em.op("scalar", lambda sc, xb=xb: sc.activation(
                    Pscr[:, 0:2048], x_t[xb][:], AF.Square), inc="sc")
                em.wait("vector", "sc", scv)
                em.op("vector", lambda v: v.reduce_sum(
                    lnb[:, 4:6],
                    Pscr[:, 0:2048].rearrange("q (s f) -> q s f", f=D),
                    axis=AX))
            em.op("vector", lambda v: v.tensor_scalar_mul(
                lnb[:, 8:10], lnb[:, 0:2], 1.0 / D))
            vpad()
            em.op("vector", lambda v: v.tensor_mul(
                lnb[:, 12:14], lnb[:, 8:10], lnb[:, 8:10]))
            vpad()
            em.op("vector", lambda v: v.scalar_tensor_tensor(
                lnb[:, 16:18], lnb[:, 4:6], 1.0 / D, lnb[:, 12:14],
                OP.mult, OP.subtract))
            vpad()
            vv = em.op("vector", lambda v: v.tensor_scalar_add(
                lnb[:, 20:22], lnb[:, 16:18], 1e-5), inc="v")
            em.wait("scalar", "v", vv)
            scv = em.op("scalar", lambda sc: sc.activation(
                lnb[:, 24:26], lnb[:, 20:22], AF.Sqrt), inc="sc")
            em.wait("vector", "sc", scv)
            em.op("vector", lambda v: v.reciprocal(lnb[:, 28:30], lnb[:, 24:26]))
            vpad()
            em.op("vector", lambda v: v.tensor_mul(
                lnb[:, 32:34], lnb[:, 28:30], lnb[:, 28:30]))
            vpad()
            em.op("vector", lambda v: v.tensor_mul(
                lnb[:, 36:38], lnb[:, 20:22], lnb[:, 32:34]))
            vpad()
            em.op("vector", lambda v: v.tensor_scalar(
                lnb[:, 40:42], lnb[:, 36:38], -0.5, 1.5, OP.mult, OP.add))
            vpad()
            em.op("vector", lambda v: v.tensor_mul(
                lnb[:, 44:46], lnb[:, 28:30], lnb[:, 40:42]))
            vpad()
            for s_i in range(NSUB):
                rb = lnb[:, 44 + s_i:45 + s_i].broadcast_to((128, D))
                fn = (lambda v, s=s_i, rb=rb, xb=xb: v.scalar_tensor_tensor(
                    ln1[:, s * D:(s + 1) * D], xv2[xb][:, s, :],
                    lnb[:, 8 + s:9 + s], rb, OP.subtract, OP.mult))
                vv = em.op("vector", fn, inc="v")
            ST["ln1_apply"][xb] = vv
            if ti + 2 < ntiles:
                em.wait("sync", "v", vv)
                for s_i in range(NSUB):
                    em.op("sync", lambda sy, s=s_i, t=ti, xb=xb: sy.dma_start(
                        out=x_t[xb][:, s * D:(s + 1) * D], in_=hx[t + 2, s]),
                        inc="x", k=16)
            em.wait("scalar", "v", vv)
            scv = em.op("scalar", lambda sc: sc.copy(hi_tm, ln1[:]), inc="sc")
            em.wait("vector", "sc", scv)
            spl = em.op("vector", lambda v: v.tensor_sub(
                lo_tm, ln1[:], hi_tm), inc="v")
            l1fv = 0
            for buf_i, (src_tm, dst_fm) in enumerate(
                    [(hi_tm, l1_hi), (lo_tm, l1_lo)]):
                for cc in range(2):
                    bank = pt_pool.acquire("tensor")
                    if buf_i == 0 and cc == 0:
                        em.wait("tensor", "sc", scv)
                    if buf_i == 1 and cc == 0:
                        em.wait("tensor", "v", spl)
                    for q in range(4):
                        c = cc * 4 + q
                        for s_i in range(NSUB):
                            fn = (lambda t, b=bank, q=q, s=s_i, c=c, src=src_tm:
                                  t.transpose(
                                      p[b][:].bitcast(BF16)[
                                          :, (q * 2 + s) * 128:(q * 2 + s + 1) * 128],
                                      src.rearrange("q (s f) -> q s f", f=D)[
                                          :, s, c * 128:(c + 1) * 128],
                                      idb_sb[:]))
                            mmv = em.op("tensor", fn,
                                        inc=("mm" if (q == 3 and s_i == 1)
                                             else None))
                    em.wait("vector", "mm", mmv)
                    dst = dst_fm.rearrange("q (c t) -> q c t", t=256)[
                        :, cc * 4:cc * 4 + 4, :]
                    src = p[bank][:].bitcast(BF16)[:, 0:1024] \
                        .rearrange("q (c t) -> q c t", t=256)
                    vv2 = em.op("vector", lambda v, d=dst, s=src:
                                v.tensor_copy(d, s), inc="v")
                    pt_pool.release(bank, "v", vv2)
                    l1fv = vv2
            ST["ln1tp_mm"] = mmv
            ST["l1f_done"][ti] = l1fv

        def emit_ffn_a(ti):
            em.ctx = f"ffna{ti}"
            conv = (ti == 0)
            h_gp = h_s = 0
            for gp in range(8):
                sl1 = slots[(2 * gp + 2) % 4]
                sl2 = slots[(2 * gp + 3) % 4]
                for sl, fsrc, zsrc in ((sl1, hw1[gp], zw1[gp]),
                                       (sl2, hw2[gp], zw2[gp])):
                    wv_ = slot_dma(sl, fsrc if conv else zsrc, dst_bf=True)
                    sl["lastwv"] = wv_
                    if conv:
                        scratch_write(sl, zsrc, wv_)
                for j in range(4):
                    oc = gp * 4 + j
                    bA = pa_pool.acquire("tensor")
                    if j == 0:
                        em.wait("tensor", sl1["sem"], sl1["lastwv"])
                        em.wait("tensor", sl2["sem"], sl2["lastwv"])
                        if gp == 0:
                            em.wait("tensor", "v", ST["l1f_done"][ti])
                            if ti > 0:
                                em.wait("tensor", "st", 16 * NSUB * ti)
                    for k in range(8):
                        for vi, rhs in ((0, l1_hi), (1, l1_lo)):
                            fn = (lambda t, b=bA, k=k, j=j, sl=sl1, r=rhs, vi=vi:
                                  t.matmul(p[b][:, 0:256],
                                           sl["bf"][:, k * 512 + j * 128:k * 512 + (j + 1) * 128],
                                           r[:, k * 256:(k + 1) * 256],
                                           start=(k == 0 and vi == 0),
                                           stop=(k == 7 and vi == 1)))
                            mmA = em.op("tensor", fn,
                                        inc=("mm" if (k == 7 and vi == 1) else None))
                    bB = pa_pool.acquire("tensor")
                    for k in range(8):
                        for vi, rhs in ((0, l1_hi), (1, l1_lo)):
                            fn = (lambda t, b=bB, k=k, j=j, sl=sl2, r=rhs, vi=vi:
                                  t.matmul(p[b][:, 0:256],
                                           sl["bf"][:, k * 512 + j * 128:k * 512 + (j + 1) * 128],
                                           r[:, k * 256:(k + 1) * 256],
                                           start=(k == 0 and vi == 0),
                                           stop=(k == 7 and vi == 1)))
                            mmB = em.op("tensor", fn,
                                        inc=("mm" if (k == 7 and vi == 1) else None))
                    if j == 3:
                        slot_done(sl1, "mm", mmB)
                        slot_done(sl2, "mm", mmB)
                    fslot = fm_pool.acquire("scalar")
                    em.wait("scalar", "mm", mmA)
                    sA = fmst[:, fslot * 256:(fslot + 1) * 256]
                    scv = em.op("scalar", lambda sc, o=sA, b=bA, oc=oc:
                                sc.activation(o, p[b][:, 0:256], AF.Silu,
                                              scale=s1_sb[:, oc:oc + 1]),
                                inc="sc")
                    pa_pool.release(bA, "sc", scv)
                    fslot2 = fm_pool.acquire("vector")
                    em.wait("vector", "mm", mmB)
                    em.wait("vector", "sc", scv)
                    hout = fmst[:, fslot2 * 256:(fslot2 + 1) * 256]
                    vv = em.op("vector", lambda v, o=hout, b=bB, oc=oc, sA=sA:
                               v.scalar_tensor_tensor(
                                   o, p[b][:, 0:256], s2_sb[:, oc:oc + 1], sA,
                                   OP.mult, OP.mult), inc="v")
                    pa_pool.release(bB, "v", vv)
                    fm_pool.release(fslot, "v", vv)
                    em.wait("scalar", "v", vv)
                    hidst = h_hi[:, oc * 256:(oc + 1) * 256]
                    scv = em.op("scalar", lambda sc, d=hidst, s=hout:
                                sc.copy(d, s), inc="sc")
                    em.wait("vector", "sc", scv)
                    lodst = h_lo[:, oc * 256:(oc + 1) * 256]
                    vv = em.op("vector", lambda v, d=lodst, s=hout, hi=hidst:
                               v.tensor_sub(d, s, hi), inc="v")
                    fm_pool.release(fslot2, "v", vv)
                    h_gp, h_s = vv, scv
            ST["hsplit"] = (h_gp, h_s)

        def emit_ffn_b(ti):
            em.ctx = f"ffnb{ti}"
            conv = (ti == 0)
            h_gp, h_s = ST["hsplit"]

            def h_stage(fsrc_arr, zsrc_arr, scale_sb, in_hi, in_lo,
                        o_hi, o_lo, gate_gp, gate_s, mid_ti=None):
                last_gp = last_s = 0
                pend = []
                for oc in range(33):
                    if mid_ti is not None and oc == 1:
                        emit_midA(mid_ti, 0)
                        em.ctx = f"ffnb{ti}"
                    if mid_ti is not None and oc == 8:
                        emit_midA(mid_ti, 1)
                        em.ctx = f"ffnb{ti}"
                    if mid_ti is not None and oc == 12:
                        emit_midB(mid_ti)
                        em.ctx = f"ffnb{ti}"
                    if mid_ti is not None and oc == 13:
                        emit_midC(mid_ti, 0)
                        em.ctx = f"ffnb{ti}"
                    if mid_ti is not None and oc == 20:
                        emit_midC(mid_ti, 1)
                        em.ctx = f"ffnb{ti}"
                    if oc < 32:
                        sl = slots[oc % 4]
                        wv_ = slot_dma(sl, fsrc_arr[oc] if conv else zsrc_arr[oc],
                                       dst_bf=True)
                        if conv:
                            scratch_write(sl, zsrc_arr[oc], wv_)
                        bank = pa_pool.acquire("tensor")
                        em.wait("tensor", sl["sem"], wv_)
                        if oc == 0:
                            em.wait("tensor", "v", gate_gp)
                            em.wait("tensor", "sc", gate_s)
                        for k in range(32):
                            for vi, rhs in ((0, in_hi), (1, in_lo)):
                                fn = (lambda t, b=bank, k=k, sl=sl, r=rhs, vi=vi:
                                      t.matmul(p[b][:, 0:256],
                                               sl["bf"][:, k * 128:(k + 1) * 128],
                                               r[:, k * 256:(k + 1) * 256],
                                               start=(k == 0 and vi == 0),
                                               stop=(k == 31 and vi == 1)))
                                mmv = em.op("tensor", fn,
                                            inc=("mm" if (k == 31 and vi == 1)
                                                 else None))
                        slot_done(sl, "mm", mmv)
                        fslot = fm_pool.acquire("scalar")
                        em.wait("scalar", "mm", mmv)
                        gbuf = fmst[:, fslot * 256:(fslot + 1) * 256]
                        scv = em.op("scalar", lambda sc, o=gbuf, b=bank, oc=oc,
                                    ssb=scale_sb: sc.activation(
                                        o, p[b][:, 0:256], AF.Gelu,
                                        scale=ssb[:, oc:oc + 1]), inc="sc")
                        pa_pool.release(bank, "sc", scv)
                        hidst = o_hi[:, oc * 256:(oc + 1) * 256]
                        scv = em.op("scalar", lambda sc, d=hidst, s=gbuf:
                                    sc.copy(d, s), inc="sc")
                        lodst = o_lo[:, oc * 256:(oc + 1) * 256]
                        pend.append((scv, lodst, gbuf, hidst, fslot))
                        last_s = scv
                    while len(pend) > 0:
                        scv_, lodst_, gbuf_, hidst_, fs_ = pend.pop(0)
                        em.wait("vector", "sc", scv_)
                        gv = em.op("vector", lambda v, d=lodst_, s=gbuf_,
                                   hi=hidst_: v.tensor_sub(d, s, hi), inc="v")
                        fm_pool.release(fs_, "v", gv)
                        last_gp = gv
                return last_gp, last_s

            g1gp, g1s = h_stage(hh1, zh1, h1s_sb, h_hi, h_lo, g1_hi, g1_lo,
                                h_gp, h_s,
                                mid_ti=(ti + 1 if ti + 1 < ntiles else None))
            g2gp, g2s = h_stage(hh2, zh2, h2s_sb, g1_hi, g1_lo, h_hi, h_lo,
                                g1gp, g1s)

            if ti + 1 < ntiles:
                em.ctx = f"tailsplit{ti + 1}"
                em.wait("scalar", "v", ST["ctx_done"][ti + 1])
                hv_ = em.op("scalar", lambda sc: sc.copy(hi_tm, ctx_tm),
                            inc="sc")
                em.wait("vector", "sc", hv_)
                lv_ = em.op("vector", lambda v: v.tensor_sub(
                    lo_tm, ctx_tm, hi_tm), inc="v")
                ST["ctx_split"] = (hv_, lv_)
                ctxfm_done = 0
                for buf_i, (src_tm, dst_fm) in enumerate(
                        [(hi_tm, xf_hi), (lo_tm, xf_lo)]):
                    for cc in range(2):
                        bank = pt_pool.acquire("tensor")
                        if buf_i == 0 and cc == 0:
                            em.wait("tensor", "sc", hv_)
                        if buf_i == 1 and cc == 0:
                            em.wait("tensor", "v", lv_)
                        for q in range(4):
                            c = cc * 4 + q
                            for s_i in range(NSUB):
                                fn = (lambda t, b=bank, q=q, s=s_i, c=c,
                                      src=src_tm: t.transpose(
                                          p[b][:].bitcast(BF16)[
                                              :, (q * 2 + s) * 128:(q * 2 + s + 1) * 128],
                                          src.rearrange("q (s f) -> q s f", f=D)[
                                              :, s, c * 128:(c + 1) * 128],
                                          idb_sb[:]))
                                mmv = em.op("tensor", fn,
                                            inc=("mm" if (q == 3 and s_i == 1)
                                                 else None))
                        em.wait("vector", "mm", mmv)
                        dst = dst_fm.rearrange("q (c t) -> q c t", t=256)[
                            :, cc * 4:cc * 4 + 4, :]
                        src = p[bank][:].bitcast(BF16)[:, 0:1024] \
                            .rearrange("q (c t) -> q c t", t=256)
                        vv_ = em.op("vector", lambda v, d=dst, s=src:
                                    v.tensor_copy(d, s), inc="v")
                        pt_pool.release(bank, "v", vv_)
                        ctxfm_done = vv_
                ST["ctxfm_done"] = ctxfm_done
                em.ctx = f"ffnb{ti}"
            for it in pt_pool.items:
                fa = pt_pool.free_at[it]
                if fa:
                    em.wait("tensor", *fa)
            em.wait("tensor", "v", g2gp)
            em.wait("tensor", "sc", g2s)
            if ST.get("w3_evac"):
                em.wait("tensor", "v", ST["w3_evac"])
            for gw in range(8):
                sl = slots[gw % 2]
                wv_ = slot_dma(sl, hw3[gw] if conv else zw3[gw], dst_bf=True)
                if conv:
                    scratch_write(sl, zw3[gw], wv_)
                em.wait("tensor", sl["sem"], wv_)
                for kk in range(4):
                    k = gw * 4 + kk
                    for src in (h_hi, h_lo):
                        for s_i in range(NSUB):
                            for nh in range(2):
                                fn = (lambda t, k=k, kk=kk, s=s_i, nh=nh,
                                      sl=sl, src=src: t.matmul(
                                          p[2 + 2 * s + nh][:],
                                          src[:, k * 256 + s * 128:k * 256 + (s + 1) * 128],
                                          sl["bf"][:, kk * 1024 + nh * 512:kk * 1024 + (nh + 1) * 512],
                                          start=(k == 0 and src is h_hi),
                                          stop=(k == 31 and src is h_lo)))
                                mmv = em.op("tensor", fn,
                                            inc=("mm" if (kk == 3 and
                                                          src is h_lo and
                                                          s_i == 1 and nh == 1)
                                                 else None))
                slot_done(sl, "mm", em.val("mm"))
            w3mm = em.val("mm")
            em.wait("vector", "mm", w3mm)
            for s_i in range(NSUB):
                for nh in range(2):
                    dst = y2[:, s_i * D + nh * 512:s_i * D + (nh + 1) * 512]
                    lsrc = ln1[:, s_i * D + nh * 512:s_i * D + (nh + 1) * 512]
                    em.op("vector", lambda v, d=dst, s=s_i, nh=nh:
                          v.tensor_tensor(d, p[2 + 2 * s + nh][:],
                                          s3_sb[:, nh * 512:(nh + 1) * 512],
                                          OP.mult))
                    vv = em.op("vector", lambda v, d=dst, ls=lsrc:
                               v.tensor_tensor(d, d, ls, OP.add), inc="v")
            w3_evac = vv
            ST["w3_evac"] = w3_evac
            for b_ in (2, 3, 4, 5):
                pa_pool.free_at[b_] = ("v", w3_evac)
            y2v = y2.rearrange("q (s f) -> q s f", f=D)
            em.op("vector", lambda v: v.reduce_sum(lnb[:, 0:2], y2v, axis=AX))
            em.wait("scalar", "v", w3_evac)
            scv = em.op("scalar", lambda sc: sc.activation(
                sq2, y2, AF.Square), inc="sc")
            em.wait("vector", "sc", scv)
            em.op("vector", lambda v: v.reduce_sum(
                lnb[:, 4:6], sq2.rearrange("q (s f) -> q s f", f=D), axis=AX))
            em.op("vector", lambda v: v.tensor_scalar_mul(
                lnb[:, 8:10], lnb[:, 0:2], 1.0 / D))
            vpad()
            em.op("vector", lambda v: v.tensor_mul(
                lnb[:, 12:14], lnb[:, 8:10], lnb[:, 8:10]))
            vpad()
            em.op("vector", lambda v: v.scalar_tensor_tensor(
                lnb[:, 16:18], lnb[:, 4:6], 1.0 / D, lnb[:, 12:14],
                OP.mult, OP.subtract))
            vpad()
            vv = em.op("vector", lambda v: v.tensor_scalar_add(
                lnb[:, 20:22], lnb[:, 16:18], 1e-5), inc="v")
            em.wait("scalar", "v", vv)
            scv = em.op("scalar", lambda sc: sc.activation(
                lnb[:, 24:26], lnb[:, 20:22], AF.Sqrt), inc="sc")
            em.wait("vector", "sc", scv)
            em.op("vector", lambda v: v.reciprocal(lnb[:, 28:30], lnb[:, 24:26]))
            vpad()
            em.op("vector", lambda v: v.tensor_mul(
                lnb[:, 32:34], lnb[:, 28:30], lnb[:, 28:30]))
            vpad()
            em.op("vector", lambda v: v.tensor_mul(
                lnb[:, 36:38], lnb[:, 20:22], lnb[:, 32:34]))
            vpad()
            em.op("vector", lambda v: v.tensor_scalar(
                lnb[:, 40:42], lnb[:, 36:38], -0.5, 1.5, OP.mult, OP.add))
            vpad()
            em.op("vector", lambda v: v.tensor_mul(
                lnb[:, 44:46], lnb[:, 28:30], lnb[:, 40:42]))
            vpad()
            for s_i in range(NSUB):
                rb = lnb[:, 44 + s_i:45 + s_i].broadcast_to((128, D))
                fn = (lambda v, s=s_i, rb=rb: v.scalar_tensor_tensor(
                    out_tm[:, s * D:(s + 1) * D], y2v[:, s, :],
                    lnb[:, 8 + s:9 + s], rb, OP.subtract, OP.mult))
                vv = em.op("vector", fn, inc="v")
            ST["out_done"] = vv
            em.wait("scalar", "v", vv)
            for s_i in range(NSUB):
                em.op("scalar", lambda sc, s=s_i, t=ti: sc.dma_start(
                    out=ho[t, s], in_=out_tm[:, s * D:(s + 1) * D]),
                    inc="st", k=16)

        emit_head(0)
        emit_mid(0)
        emit_tail(0)
        for i in range(ntiles):
            if i + 1 < ntiles:
                emit_head(i + 1)
            emit_ffn_a(i)
            emit_ffn_b(i)
            if i + 1 < ntiles:
                emit_tail(i + 1)
        em.wait("scalar", "st", em.val("st"))

        def replay(eng_name):
            def run(e):
                for item in em.ops[eng_name]:
                    if item[0] == "w":
                        e.wait_ge(sems[item[1]], item[2])
                    else:
                        _, fn, incn, k = item[:4]
                        instr = fn(e)
                        if incn is not None:
                            instr.then_inc(sems[incn], k)
            return run

        blk.sync(replay("sync"))
        blk.gpsimd(replay("gpsimd"))
        blk.tensor(replay("tensor"))
        blk.vector(replay("vector"))
        blk.scalar(replay("scalar"))

    globals()['LAST_EM'] = em
    return nc


def _prep_weights(inputs):
    f32 = np.float32

    def c(a):
        return np.ascontiguousarray(a, dtype=f32)

    Wq = np.asarray(inputs["L0_attn_W_qkv"], dtype=f32)
    Wo = np.asarray(inputs["L0_attn_W_out"], dtype=f32)
    w1T = np.asarray(inputs["L0_ffn_w1_w"], dtype=f32).T
    w2T = np.asarray(inputs["L0_ffn_w2_w"], dtype=f32).T
    h1T = np.asarray(inputs["L0_ffn_h1_w"], dtype=f32).T
    h2T = np.asarray(inputs["L0_ffn_h2_w"], dtype=f32).T
    w3T = np.asarray(inputs["L0_ffn_w3_w"], dtype=f32).T
    d = {
        "wq": c(Wq.reshape(8, 128, 12, 256).transpose(2, 1, 0, 3)
                .reshape(12, 128, 2048)),
        "wo": c(Wo.reshape(8, 128, 4, 256).transpose(2, 1, 0, 3)
                .reshape(4, 128, 2048)),
        "w1": c(w1T.reshape(8, 128, 8, 512).transpose(2, 1, 0, 3)
                .reshape(8, 128, 4096)),
        "w2": c(w2T.reshape(8, 128, 8, 512).transpose(2, 1, 0, 3)
                .reshape(8, 128, 4096)),
        "h1": c(h1T.reshape(32, 128, 32, 128).transpose(2, 1, 0, 3)
                .reshape(32, 128, 4096)),
        "h2": c(h2T.reshape(32, 128, 32, 128).transpose(2, 1, 0, 3)
                .reshape(32, 128, 4096)),
        "w3": c(w3T.reshape(8, 4, 128, 1024).transpose(0, 2, 1, 3)
                .reshape(8, 128, 4096)),
        "bqbc": c(np.broadcast_to(
            np.asarray(inputs["L0_attn_b_qkv"], dtype=f32), (128, QKVD))),
        "bobc": c(np.broadcast_to(
            np.asarray(inputs["L0_attn_b_out"], dtype=f32), (128, D))),
        "s1": c(np.asarray(inputs["L0_ffn_w1_s"], dtype=f32)
                .reshape(32, 128).T),
        "s2": c(np.asarray(inputs["L0_ffn_w2_s"], dtype=f32)
                .reshape(32, 128).T),
        "h1s": c(np.asarray(inputs["L0_ffn_h1_s"], dtype=f32)
                 .reshape(32, 128).T),
        "h2s": c(np.asarray(inputs["L0_ffn_h2_s"], dtype=f32)
                 .reshape(32, 128).T),
        "s3bc": c(np.broadcast_to(
            np.asarray(inputs["L0_ffn_w3_s"], dtype=f32), (128, D))),
        "idf": np.eye(128, dtype=f32),
        "idb": np.eye(128, dtype=ml_dtypes.bfloat16),
    }
    return d


def kernel(**inputs):
    x = np.ascontiguousarray(
        np.asarray(inputs["L0_attn_x"], dtype=np.float32).reshape(-1, D))
    ntok = x.shape[0]
    per = ntok // NCORES
    wd = _prep_weights(inputs)
    nc = build_nc(NTILES)
    in_maps = []
    for ci in range(NCORES):
        m = dict(wd)
        m["x"] = np.ascontiguousarray(
            x[ci * per:(ci + 1) * per].reshape(NTILES, NSUB, 128, D))
        in_maps.append(m)
    res = run_bass_kernel_spmd(nc, in_maps, core_ids=list(range(NCORES)))
    out = np.concatenate(
        [np.asarray(r["out"]).reshape(per, D) for r in res.results], axis=0)
    return out.reshape(np.asarray(inputs["L0_attn_x"]).shape)


# version 15 (final)
# speedup vs baseline: 1.0628x; 1.0628x over previous
import numpy as np
import ml_dtypes
import concourse.bass as bass
import concourse.mybir as mybir
from concourse.bass_utils import run_bass_kernel_spmd
from contextlib import ExitStack

F32 = mybir.dt.float32
BF16 = mybir.dt.bfloat16
AF = mybir.ActivationFunctionType
OP = mybir.AluOpType
AX = mybir.AxisListType.X

NCORES = 8
D = 1024
QKVD = 3072
DH = 4096
T = 256
NSUB = 2
NTILES = 16
ENGS = ("sync", "gpsimd", "tensor", "vector", "scalar")


class Em:
    def __init__(self):
        self.ops = {e: [] for e in ENGS}
        self.cnt = {}
        self.ctx = ""

    def wait(self, eng, sem, val):
        if val > 0:
            self.ops[eng].append(("w", sem, val, self.ctx))

    def op(self, eng, fn, inc=None, k=1):
        if inc is not None:
            self.cnt[inc] = self.cnt.get(inc, 0) + k
        self.ops[eng].append(("o", fn, inc, k, self.ctx))
        return self.cnt.get(inc, 0)

    def val(self, sem):
        return self.cnt.get(sem, 0)


class Pool:
    def __init__(self, em, items):
        self.em = em
        self.items = list(items)
        self.free_at = {i: None for i in items}
        self.rr = 0

    def acquire(self, eng):
        it = self.items[self.rr % len(self.items)]
        self.rr += 1
        fa = self.free_at[it]
        if fa is not None:
            self.em.wait(eng, fa[0], fa[1])
        return it

    def release(self, it, sem, val):
        self.free_at[it] = (sem, val)


def build_nc(ntiles=NTILES):
    nc = bass.Bass("TRN2", target_bir_lowering=False, debug=False)
    hx = nc.dram_tensor("x", [ntiles, NSUB, 128, D], F32, kind="ExternalInput").ap()
    ho = nc.dram_tensor("out", [ntiles, NSUB, 128, D], F32, kind="ExternalOutput").ap()
    hwq = nc.dram_tensor("wq", [12, 128, 2048], F32, kind="ExternalInput").ap()
    hwo = nc.dram_tensor("wo", [4, 128, 2048], F32, kind="ExternalInput").ap()
    hw1 = nc.dram_tensor("w1", [8, 128, 4096], F32, kind="ExternalInput").ap()
    hw2 = nc.dram_tensor("w2", [8, 128, 4096], F32, kind="ExternalInput").ap()
    hh1 = nc.dram_tensor("h1", [32, 128, 4096], F32, kind="ExternalInput").ap()
    hh2 = nc.dram_tensor("h2", [32, 128, 4096], F32, kind="ExternalInput").ap()
    hw3 = nc.dram_tensor("w3", [8, 128, 4096], F32, kind="ExternalInput").ap()
    hbq = nc.dram_tensor("bqbc", [128, QKVD], F32, kind="ExternalInput").ap()
    hbo = nc.dram_tensor("bobc", [128, D], F32, kind="ExternalInput").ap()
    hs1 = nc.dram_tensor("s1", [128, 32], F32, kind="ExternalInput").ap()
    hs2 = nc.dram_tensor("s2", [128, 32], F32, kind="ExternalInput").ap()
    hh1s = nc.dram_tensor("h1s", [128, 32], F32, kind="ExternalInput").ap()
    hh2s = nc.dram_tensor("h2s", [128, 32], F32, kind="ExternalInput").ap()
    hs3 = nc.dram_tensor("s3bc", [128, D], F32, kind="ExternalInput").ap()
    hidf = nc.dram_tensor("idf", [128, 128], F32, kind="ExternalInput").ap()
    hidb = nc.dram_tensor("idb", [128, 128], BF16, kind="ExternalInput").ap()
    zw1 = nc.dram_tensor("zw1", [8, 128, 4096], BF16, kind="Internal").ap()
    zw2 = nc.dram_tensor("zw2", [8, 128, 4096], BF16, kind="Internal").ap()
    zh1 = nc.dram_tensor("zh1", [32, 128, 4096], BF16, kind="Internal").ap()
    zh2 = nc.dram_tensor("zh2", [32, 128, 4096], BF16, kind="Internal").ap()
    zw3 = nc.dram_tensor("zw3", [8, 128, 4096], BF16, kind="Internal").ap()
    zwq = nc.dram_tensor("zwq", [12, 128, 4096], BF16, kind="Internal").ap()
    zwo = nc.dram_tensor("zwo", [4, 128, 4096], BF16, kind="Internal").ap()

    with ExitStack() as st:
        def sb(name, shape, dt=F32):
            return st.enter_context(nc.sbuf_tensor(name, shape, dt))

        regA = sb("regA", [128, 8192])
        regB = sb("regB", [128, 8192])
        qkvt = sb("qkvt", [128, 6144])
        patt = sb("patt", [128, 2048])
        ctxa = sb("ctxa", [128, 3392])
        x_t = [sb("x_t0", [128, 2048]), sb("x_t1", [128, 2048])]
        xfm = sb("xfm", [128, 2048])
        ring = sb("ring", [128, 8192])
        ln1 = sb("ln1", [128, 2048])
        l1f = sb("l1f", [128, 2048])
        fmst = sb("fmst", [128, 2048])
        lnb = sb("lnb", [128, 64])
        s1_sb = sb("s1_sb", [128, 32])
        s2_sb = sb("s2_sb", [128, 32])
        h1s_sb = sb("h1s_sb", [128, 32])
        h2s_sb = sb("h2s_sb", [128, 32])
        s3_sb = sb("s3_sb", [128, D])
        idf_sb = sb("idf_sb", [128, 128])
        idb_sb = sb("idb_sb", [128, 128], BF16)
        p = [st.enter_context(nc.psum_tensor(f"p{i}", [128, 512], F32))
             for i in range(8)]

        sem_names = ["cst", "x", "st", "mm", "v", "sc", "bs", "sw", "gp",
                     "sl0", "sl1", "sl2", "sl3"]
        sems = {n: st.enter_context(nc.semaphore(n)) for n in sem_names}
        blk = st.enter_context(nc.Block())

        A = regA[:]
        B = regB[:]
        Ab = A.bitcast(BF16)
        Bb = B.bitcast(BF16)
        pb = patt[:].bitcast(BF16)

        h_hi = Ab[:, 0:8192]
        h_lo = Ab[:, 8192:16384]
        g1_hi = Bb[:, 0:8192]
        g1_lo = Bb[:, 8192:16384]
        out_tm = B[:, 0:2048]
        sq2 = B[:, 2048:4096]
        y2 = B[:, 4096:6144]
        bobc = B[:, 2048:3072]
        bqbc = B[:, 3072:6144]
        Pscr = patt[:]
        hi_tm = pb[:, 0:2048]
        lo_tm = pb[:, 2048:4096]
        ctx_tm = ctxa[:, 0:2048]
        S_s = [ctxa[:, 2048:2304], ctxa[:, 2304:2560]]
        E_s = [ctxa[:, 2560:2816], ctxa[:, 2816:3072]]
        A_b = ctxa[:, 3072:3328]
        M_s = [ctxa[:, 3328:3344], ctxa[:, 3344:3360]]
        R_b = ctxa[:, 3360:3376]
        Rr_b = ctxa[:, 3376:3392]
        l1_hi = l1f[:].bitcast(BF16)[:, 0:2048]
        l1_lo = l1f[:].bitcast(BF16)[:, 2048:4096]
        xf_hi = xfm[:].bitcast(BF16)[:, 0:2048]
        xf_lo = xfm[:].bitcast(BF16)[:, 2048:4096]


        slots = []
        for i in range(4):
            f32v = ring[:, i * 2048:(i + 1) * 2048]
            bfv = ring[:].bitcast(BF16)[:, i * 4096:(i + 1) * 4096]
            slots.append({"f32": f32v, "bf": bfv, "sem": f"sl{i}", "n": 0,
                          "idx": i})

        em = Em()

        def vpad(n=8):
            for _ in range(n):
                em.op("vector", lambda v: v.engine_nop())

        pa_pool = Pool(em, [0, 1, 2, 3, 4, 5])
        pt_pool = Pool(em, [6, 7])
        fm_pool = Pool(em, list(range(8)))
        slot_free = {i: [] for i in range(4)}

        def slot_dma(slot, src_ap, dst_bf, eng="gpsimd"):
            i = slot["idx"]
            for w_ in slot_free[i]:
                em.wait(eng, *w_)
            slot_free[i] = []
            dst = slot["bf"] if dst_bf else slot["f32"]
            em.op(eng, lambda g, d=dst, s=src_ap: g.dma_start(out=d, in_=s),
                  inc=slot["sem"], k=16)
            slot["n"] += 1
            return 16 * slot["n"]

        def slot_done(slot, sem, val):
            slot_free[slot["idx"]].append((sem, val))

        def scratch_write(slot, dst_ap, wv_):
            em.wait("scalar", slot["sem"], wv_)
            swv = em.op("scalar", lambda sc, d=dst_ap, s=slot["bf"]:
                        sc.dma_start(out=d, in_=s), inc="sw", k=16)
            slot_done(slot, "sw", swv)

        stg_free = {2: 0, 3: 0}

        def split_write(slot, wv_, dst_ap, si):
            stg = slots[2 + (si & 1)]
            em.wait("scalar", slot["sem"], wv_)
            if stg_free[stg["idx"]]:
                em.wait("scalar", "sw", stg_free[stg["idx"]])
            hv = em.op("scalar", lambda sc, d=stg["bf"], s=slot["f32"]:
                       sc.copy(d[:, 0:2048], s), inc="sc")
            em.wait("vector", "sc", hv)
            lv = em.op("vector", lambda v, d=stg["bf"], s=slot["f32"]:
                       v.tensor_sub(d[:, 2048:4096], s, d[:, 0:2048]), inc="v")
            em.wait("scalar", "v", lv)
            swv = em.op("scalar", lambda sc, d=dst_ap, s=stg["bf"]:
                        sc.dma_start(out=d, in_=s), inc="sw", k=16)
            stg_free[stg["idx"]] = swv
            slot_free[stg["idx"]].append(("sw", swv))
            slot_free[slot["idx"]].append(("v", lv))

        consts = [(s1_sb[:], hs1), (s2_sb[:], hs2),
                  (h1s_sb[:], hh1s), (h2s_sb[:], hh2s), (s3_sb[:], hs3),
                  (idf_sb[:], hidf), (idb_sb[:], hidb)]
        for d_, s_ in consts:
            em.op("sync", lambda sy, d=d_, s=s_: sy.dma_start(out=d, in_=s),
                  inc="cst", k=16)
        cst_all = em.val("cst")
        for t_ in range(min(2, ntiles)):
            for s_ in range(NSUB):
                em.op("sync", lambda sy, t=t_, s=s_: sy.dma_start(
                    out=x_t[t][:, s * D:(s + 1) * D], in_=hx[t, s]),
                    inc="x", k=16)

        ST = {
            "ln1_apply": [0, 0],
            "outproj_mm": 0,
            "ln1tp_mm": 0,
            "out_done": 0,
            "ctx_done": {},
            "l1f_done": {},
            "split_done": {},
        }

        xv2 = [x_t[0][:].rearrange("q (s f) -> q s f", f=D),
               x_t[1][:].rearrange("q (s f) -> q s f", f=D)]
        qkvv = qkvt[:].rearrange("q (s f) -> q s f", f=QKVD)

        def emit_head(ti):
            em.ctx = f"head{ti}"
            xb = ti % 2
            if ti == 0:
                em.wait("tensor", "cst", cst_all)
                em.wait("vector", "cst", cst_all)
                em.wait("scalar", "cst", cst_all)
            if ST["out_done"]:
                em.wait("sync", "v", ST["out_done"])
            em.wait("sync", "mm", em.val("mm"))
            em.op("sync", lambda sy: sy.dma_start(out=bobc, in_=hbo),
                  inc="bs", k=16)
            em.op("sync", lambda sy: sy.dma_start(out=bqbc, in_=hbq),
                  inc="bs", k=16)
            bias_v = em.val("bs")
            em.wait("tensor", "x", 16 * NSUB * (ti + 1))
            xe = 0
            if ti == 0:
                for grp in range(4):
                    s_i, cc = grp // 2, grp % 2
                    bank = pt_pool.acquire("tensor")
                    for q in range(4):
                        c = cc * 4 + q
                        fn = (lambda t, b=bank, q=q, s=s_i, c=c, xb=xb:
                              t.transpose(p[b][:, q * 128:(q + 1) * 128],
                                          x_t[xb][:, s * D + c * 128:s * D + (c + 1) * 128],
                                          idf_sb[:]))
                        mmv = em.op("tensor", fn, inc=("mm" if q == 3 else None))
                    em.wait("vector", "mm", mmv)
                    dst = xfm[:].rearrange("q (c t) -> q c t", t=256)[
                        :, cc * 4:cc * 4 + 4, s_i * 128:(s_i + 1) * 128]
                    src = p[bank][:, 0:512].rearrange("q (c t) -> q c t", t=128)
                    vv = em.op("vector", lambda v, d=dst, s=src:
                               v.tensor_copy(d, s), inc="v")
                    pt_pool.release(bank, "v", vv)
                    xe = vv
            else:
                if ST["ln1tp_mm"]:
                    em.wait("scalar", "mm", ST["ln1tp_mm"])
                hv = em.op("scalar", lambda sc, xb=xb:
                           sc.copy(hi_tm, x_t[xb][:]), inc="sc")
                em.wait("vector", "sc", hv)
                lv = em.op("vector", lambda v, xb=xb: v.tensor_sub(
                    lo_tm, x_t[xb][:], hi_tm), inc="v")
                for buf_i, (src_tm, dst_fm) in enumerate(
                        [(hi_tm, xf_hi), (lo_tm, xf_lo)]):
                    for cc in range(2):
                        bank = pt_pool.acquire("tensor")
                        if buf_i == 0 and cc == 0:
                            em.wait("tensor", "sc", hv)
                        if buf_i == 1 and cc == 0:
                            em.wait("tensor", "v", lv)
                        for q in range(4):
                            c = cc * 4 + q
                            for s_i in range(NSUB):
                                fn = (lambda t, b=bank, q=q, s=s_i, c=c,
                                      src=src_tm: t.transpose(
                                          p[b][:].bitcast(BF16)[
                                              :, (q * 2 + s) * 128:(q * 2 + s + 1) * 128],
                                          src.rearrange("q (s f) -> q s f", f=D)[
                                              :, s, c * 128:(c + 1) * 128],
                                          idb_sb[:]))
                                mmv = em.op("tensor", fn,
                                            inc=("mm" if (q == 3 and s_i == 1)
                                                 else None))
                        em.wait("vector", "mm", mmv)
                        if buf_i == 0 and cc == 0 and ST["outproj_mm"]:
                            em.wait("vector", "mm", ST["outproj_mm"])
                        dst = dst_fm.rearrange("q (c t) -> q c t", t=256)[
                            :, cc * 4:cc * 4 + 4, :]
                        src = p[bank][:].bitcast(BF16)[:, 0:1024] \
                            .rearrange("q (c t) -> q c t", t=256)
                        vv = em.op("vector", lambda v, d=dst, s=src:
                                   v.tensor_copy(d, s), inc="v")
                        pt_pool.release(bank, "v", vv)
                        xe = vv
            ST["xtp_mm"] = mmv
            em.wait("vector", "bs", bias_v)
            for s_i in range(NSUB):
                fn_ = (lambda v, s=s_i, xb=xb: v.tensor_tensor(
                    x_t[xb][:, s * D:(s + 1) * D],
                    x_t[xb][:, s * D:(s + 1) * D], bobc[:], OP.add))
                em.op("vector", fn_, inc="v")
            for g in range(12):
                slot = slots[g % 2]
                if ti == 0:
                    wv_ = slot_dma(slot, hwq[g], dst_bf=False, eng="sync")
                    split_write(slot, wv_, zwq[g], g)
                else:
                    wv_ = slot_dma(slot, zwq[g], dst_bf=True)
                last_mm = 0
                for s_i in range(NSUB):
                    bank = pa_pool.acquire("tensor")
                    if s_i == 0:
                        em.wait("tensor", slot["sem"], wv_)
                        em.wait("tensor", "v", xe)
                    if ti == 0:
                        for k in range(8):
                            fn = (lambda t, b=bank, k=k, s=s_i, sl=slot:
                                  t.matmul(p[b][:, 0:256],
                                           xfm[:, k * 256 + s * 128:k * 256 + (s + 1) * 128],
                                           sl["f32"][:, k * 256:(k + 1) * 256],
                                           start=(k == 0), stop=(k == 7)))
                            mmv = em.op("tensor", fn,
                                        inc=("mm" if k == 7 else None))
                    else:
                        passes = [(xf_hi, 0), (xf_hi, 2048), (xf_lo, 0)]
                        for pi, (xbuf, roff) in enumerate(passes):
                            for k in range(8):
                                fn = (lambda t, b=bank, k=k, s=s_i, sl=slot,
                                      xbuf=xbuf, roff=roff, pi=pi: t.matmul(
                                          p[b][:, 0:256],
                                          xbuf[:, k * 256 + s * 128:k * 256 + (s + 1) * 128],
                                          sl["bf"][:, roff + k * 256:roff + (k + 1) * 256],
                                          start=(pi == 0 and k == 0),
                                          stop=(pi == 2 and k == 7)))
                                mmv = em.op("tensor", fn,
                                            inc=("mm" if (pi == 2 and k == 7)
                                                 else None))
                    last_mm = mmv
                    em.wait("vector", "mm", mmv)
                    dst = qkvv[:, s_i, g * 256:(g + 1) * 256]
                    vv = em.op("vector", lambda v, d=dst, b=bank, g=g:
                               v.tensor_tensor(d, p[b][:, 0:256],
                                               bqbc[:, g * 256:(g + 1) * 256],
                                               OP.add), inc="v")
                    pa_pool.release(bank, "v", vv)
                slot_done(slot, "mm", last_mm)

        def emit_midA(ti, s_i):
            em.ctx = f"mid{ti}"
            if s_i == 0 and ST.get("xtp_mm"):
                em.wait("vector", "mm", ST["xtp_mm"])
            Pv = Pscr.rearrange("q (h g d) -> q h g d", g=16, d=64)
            S_b, M_b = S_s[s_i], M_s[s_i]
            qk = qkvv[:, s_i, :].rearrange("q (h j) -> q h j", j=192)
            for hf in range(8):
                qa = qk[:, 2 * hf:2 * (hf + 1), 0:64].unsqueeze(2) \
                    .broadcast_to((128, 2, 16, 64))
                ka = qk[:, 0:16, 64:128].unsqueeze(1) \
                    .broadcast_to((128, 2, 16, 64))
                em.op("vector", lambda v, qa=qa, ka=ka:
                      v.tensor_tensor(Pv[:, 0:2], qa, ka, OP.mult))
                sdst = S_b[:, 32 * hf:32 * (hf + 1)] \
                    .rearrange("q (h g) -> q h g", g=16)
                em.op("vector", lambda v, d=sdst:
                      v.reduce_sum(d, Pv[:, 0:2], axis=AX))
            Sv = S_b.rearrange("q (h g) -> q h g", g=16)
            em.op("vector", lambda v, S=Sv, M=M_b: v.reduce_max(M, S, axis=AX))
            vpad(4)
            mbc = M_b.unsqueeze(2).broadcast_to((128, 16, 16))
            zv = em.op("vector", lambda v, S=Sv, m=mbc:
                       v.tensor_tensor(S, S, m, OP.subtract), inc="v")
            ST[f"zv{s_i}"] = zv

        def emit_midB(ti):
            em.ctx = f"mid{ti}"
            for s_i in range(NSUB):
                em.wait("scalar", "v", ST[f"zv{s_i}"])
                ev = em.op("scalar", lambda sc, s=s_i: sc.activation(
                    E_s[s], S_s[s], AF.Exp, scale=0.125), inc="sc")
                ST[f"ev{s_i}"] = ev

        def emit_midC(ti, s_i):
            em.ctx = f"mid{ti}"
            P2v = Pscr.rearrange("q (h d g) -> q h d g", d=64, g=16)
            E_b = E_s[s_i]
            qk = qkvv[:, s_i, :].rearrange("q (h j) -> q h j", j=192)
            em.wait("vector", "sc", ST[f"ev{s_i}"])
            Ev = E_b.rearrange("q (h g) -> q h g", g=16)
            em.op("vector", lambda v, E=Ev: v.reduce_sum(R_b, E, axis=AX))
            vpad(6)
            em.op("vector", lambda v: v.reciprocal(Rr_b, R_b))
            vpad(6)
            rbc = Rr_b.unsqueeze(2).broadcast_to((128, 16, 16))
            em.op("vector", lambda v, r=rbc, E=Ev: v.tensor_tensor(
                A_b.rearrange("q (h g) -> q h g", g=16), E, r, OP.mult))
            last = 0
            for hf in range(8):
                Aex = A_b.rearrange("q (h g) -> q h g", g=16)[
                    :, 2 * hf:2 * (hf + 1), :].unsqueeze(2) \
                    .broadcast_to((128, 2, 64, 16))
                Vex = qk[:, 0:16, 128:192].transpose([0, 2, 1]) \
                    .unsqueeze(1).broadcast_to((128, 2, 64, 16))
                em.op("vector", lambda v, a=Aex, w=Vex:
                      v.tensor_tensor(P2v[:, 0:2], a, w, OP.mult))
                cdst = ctx_tm.rearrange("q (s f) -> q s f", f=D)[
                    :, s_i, 128 * hf:128 * (hf + 1)] \
                    .rearrange("q (h d) -> q h d", d=64)
                last = em.op("vector", lambda v, d=cdst:
                             v.reduce_sum(d, P2v[:, 0:2], axis=AX), inc="v")
            if s_i == NSUB - 1:
                ST["ctx_done"][ti] = last

        def emit_mid(ti):
            emit_midA(ti, 0)
            emit_midA(ti, 1)
            emit_midB(ti)
            emit_midC(ti, 0)
            emit_midC(ti, 1)

        def emit_tail(ti):
            em.ctx = f"tail{ti}"
            xb = ti % 2
            ctxfm_done = 0
            if ti == 0:
                for s_i in range(NSUB):
                    for cc in range(2):
                        bank = pt_pool.acquire("tensor")
                        em.wait("tensor", "v", ST["ctx_done"][ti])
                        for q in range(4):
                            c = cc * 4 + q
                            fn = (lambda t, b=bank, q=q, s=s_i, c=c: t.transpose(
                                p[b][:, q * 128:(q + 1) * 128],
                                ctx_tm.rearrange("q (s f) -> q s f", f=D)[
                                    :, s, c * 128:(c + 1) * 128],
                                idf_sb[:]))
                            mmv = em.op("tensor", fn,
                                        inc=("mm" if q == 3 else None))
                        em.wait("vector", "mm", mmv)
                        dst = xfm[:].rearrange("q (c t) -> q c t", t=256)[
                            :, cc * 4:cc * 4 + 4, s_i * 128:(s_i + 1) * 128]
                        src = p[bank][:, 0:512].rearrange("q (c t) -> q c t",
                                                          t=128)
                        vv = em.op("vector", lambda v, d=dst, s=src:
                                   v.tensor_copy(d, s), inc="v")
                        pt_pool.release(bank, "v", vv)
                        ctxfm_done = vv
            else:
                ctxfm_done = ST["ctxfm_done"]
            if ti == 0:
                for g in range(4):
                    slot = slots[g % 2]
                    wv_ = slot_dma(slot, hwo[g], dst_bf=False, eng="sync")
                    split_write(slot, wv_, zwo[g], g)
                    last_mm = 0
                    for s_i in range(NSUB):
                        bank = pa_pool.acquire("tensor")
                        if s_i == 0:
                            em.wait("tensor", slot["sem"], wv_)
                            em.wait("tensor", "v", ctxfm_done)
                        for k in range(8):
                            fn = (lambda t, b=bank, k=k, s=s_i, sl=slot:
                                  t.matmul(p[b][:, 0:256],
                                           xfm[:, k * 256 + s * 128:k * 256 + (s + 1) * 128],
                                           sl["f32"][:, k * 256:(k + 1) * 256],
                                           start=(k == 0), stop=(k == 7)))
                            mmv = em.op("tensor", fn,
                                        inc=("mm" if k == 7 else None))
                        last_mm = mmv
                        em.wait("vector", "mm", mmv)
                        dst = xv2[xb][:, s_i, g * 256:(g + 1) * 256]
                        vv = em.op("vector", lambda v, d=dst, b=bank:
                                   v.tensor_tensor(d, p[b][:, 0:256], d,
                                                   OP.add), inc="v")
                        pa_pool.release(bank, "v", vv)
                    slot_done(slot, "mm", last_mm)
                resid_s = [0, 0]
                resid_s[NSUB - 1] = vv
                resid_done = vv
            else:
                wo_wv = {}
                for g in range(4):
                    wo_wv[g] = slot_dma(slots[g], zwo[g], dst_bf=True)
                resid_s = [0, 0]
                last_mm = 0
                for s_i in range(NSUB):
                    for g in range(4):
                        slot = slots[g]
                        bank = pa_pool.acquire("tensor")
                        if s_i == 0:
                            em.wait("tensor", slot["sem"], wo_wv[g])
                            if g == 0:
                                em.wait("tensor", "v", ctxfm_done)
                        passes = [(xf_hi, 0), (xf_hi, 2048), (xf_lo, 0)]
                        for pi, (xbuf, roff) in enumerate(passes):
                            for k in range(8):
                                fn = (lambda t, b=bank, k=k, s=s_i, sl=slot,
                                      xbuf=xbuf, roff=roff, pi=pi: t.matmul(
                                          p[b][:, 0:256],
                                          xbuf[:, k * 256 + s * 128:k * 256 + (s + 1) * 128],
                                          sl["bf"][:, roff + k * 256:roff + (k + 1) * 256],
                                          start=(pi == 0 and k == 0),
                                          stop=(pi == 2 and k == 7)))
                                mmv = em.op("tensor", fn,
                                            inc=("mm" if (pi == 2 and k == 7)
                                                 else None))
                        last_mm = mmv
                        if s_i == NSUB - 1:
                            slot_done(slot, "mm", mmv)
                        em.wait("vector", "mm", mmv)
                        dst = xv2[xb][:, s_i, g * 256:(g + 1) * 256]
                        vv = em.op("vector", lambda v, d=dst, b=bank:
                                   v.tensor_tensor(d, p[b][:, 0:256], d,
                                                   OP.add), inc="v")
                        pa_pool.release(bank, "v", vv)
                    resid_s[s_i] = vv
                    em.op("vector", lambda v, xb=xb, s=s_i: v.reduce_sum(
                        lnb[:, s:s + 1], xv2[xb][:, s:s + 1, :], axis=AX))
                    em.wait("scalar", "v", vv)
                    scv0 = em.op("scalar", lambda sc, xb=xb, s=s_i:
                                 sc.activation(
                                     Pscr[:, s * D:(s + 1) * D],
                                     x_t[xb][:, s * D:(s + 1) * D],
                                     AF.Square), inc="sc")
                    em.wait("vector", "sc", scv0)
                    em.op("vector", lambda v, s=s_i: v.reduce_sum(
                        lnb[:, 4 + s:5 + s],
                        Pscr[:, s * D:(s + 1) * D].unsqueeze(1), axis=AX))
                resid_done = vv
            ST["outproj_mm"] = last_mm
            if ti == 0:
                em.op("vector", lambda v, xb=xb: v.reduce_sum(
                    lnb[:, 0:2], xv2[xb], axis=AX))
                em.wait("scalar", "v", resid_done)
                scv = em.op("scalar", lambda sc, xb=xb: sc.activation(
                    Pscr[:, 0:2048], x_t[xb][:], AF.Square), inc="sc")
                em.wait("vector", "sc", scv)
                em.op("vector", lambda v: v.reduce_sum(
                    lnb[:, 4:6],
                    Pscr[:, 0:2048].rearrange("q (s f) -> q s f", f=D),
                    axis=AX))
            em.op("vector", lambda v: v.tensor_scalar_mul(
                lnb[:, 8:10], lnb[:, 0:2], 1.0 / D))
            vpad()
            em.op("vector", lambda v: v.tensor_mul(
                lnb[:, 12:14], lnb[:, 8:10], lnb[:, 8:10]))
            vpad()
            em.op("vector", lambda v: v.scalar_tensor_tensor(
                lnb[:, 16:18], lnb[:, 4:6], 1.0 / D, lnb[:, 12:14],
                OP.mult, OP.subtract))
            vpad()
            vv = em.op("vector", lambda v: v.tensor_scalar_add(
                lnb[:, 20:22], lnb[:, 16:18], 1e-5), inc="v")
            em.wait("scalar", "v", vv)
            scv = em.op("scalar", lambda sc: sc.activation(
                lnb[:, 24:26], lnb[:, 20:22], AF.Sqrt), inc="sc")
            em.wait("vector", "sc", scv)
            em.op("vector", lambda v: v.reciprocal(lnb[:, 28:30], lnb[:, 24:26]))
            vpad()
            em.op("vector", lambda v: v.tensor_mul(
                lnb[:, 32:34], lnb[:, 28:30], lnb[:, 28:30]))
            vpad()
            em.op("vector", lambda v: v.tensor_mul(
                lnb[:, 36:38], lnb[:, 20:22], lnb[:, 32:34]))
            vpad()
            em.op("vector", lambda v: v.tensor_scalar(
                lnb[:, 40:42], lnb[:, 36:38], -0.5, 1.5, OP.mult, OP.add))
            vpad()
            em.op("vector", lambda v: v.tensor_mul(
                lnb[:, 44:46], lnb[:, 28:30], lnb[:, 40:42]))
            vpad()
            for s_i in range(NSUB):
                rb = lnb[:, 44 + s_i:45 + s_i].broadcast_to((128, D))
                fn = (lambda v, s=s_i, rb=rb, xb=xb: v.scalar_tensor_tensor(
                    ln1[:, s * D:(s + 1) * D], xv2[xb][:, s, :],
                    lnb[:, 8 + s:9 + s], rb, OP.subtract, OP.mult))
                vv = em.op("vector", fn, inc="v")
            ST["ln1_apply"][xb] = vv
            if ti + 2 < ntiles:
                em.wait("sync", "v", vv)
                for s_i in range(NSUB):
                    em.op("sync", lambda sy, s=s_i, t=ti, xb=xb: sy.dma_start(
                        out=x_t[xb][:, s * D:(s + 1) * D], in_=hx[t + 2, s]),
                        inc="x", k=16)
            em.wait("scalar", "v", vv)
            scv = em.op("scalar", lambda sc: sc.copy(hi_tm, ln1[:]), inc="sc")
            em.wait("vector", "sc", scv)
            spl = em.op("vector", lambda v: v.tensor_sub(
                lo_tm, ln1[:], hi_tm), inc="v")
            l1fv = 0
            for buf_i, (src_tm, dst_fm) in enumerate(
                    [(hi_tm, l1_hi), (lo_tm, l1_lo)]):
                for cc in range(2):
                    bank = pt_pool.acquire("tensor")
                    if buf_i == 0 and cc == 0:
                        em.wait("tensor", "sc", scv)
                    if buf_i == 1 and cc == 0:
                        em.wait("tensor", "v", spl)
                    for q in range(4):
                        c = cc * 4 + q
                        for s_i in range(NSUB):
                            fn = (lambda t, b=bank, q=q, s=s_i, c=c, src=src_tm:
                                  t.transpose(
                                      p[b][:].bitcast(BF16)[
                                          :, (q * 2 + s) * 128:(q * 2 + s + 1) * 128],
                                      src.rearrange("q (s f) -> q s f", f=D)[
                                          :, s, c * 128:(c + 1) * 128],
                                      idb_sb[:]))
                            mmv = em.op("tensor", fn,
                                        inc=("mm" if (q == 3 and s_i == 1)
                                             else None))
                    em.wait("vector", "mm", mmv)
                    dst = dst_fm.rearrange("q (c t) -> q c t", t=256)[
                        :, cc * 4:cc * 4 + 4, :]
                    src = p[bank][:].bitcast(BF16)[:, 0:1024] \
                        .rearrange("q (c t) -> q c t", t=256)
                    vv2 = em.op("vector", lambda v, d=dst, s=src:
                                v.tensor_copy(d, s), inc="v")
                    pt_pool.release(bank, "v", vv2)
                    l1fv = vv2
            ST["ln1tp_mm"] = mmv
            ST["l1f_done"][ti] = l1fv

        def emit_ffn_a(ti):
            em.ctx = f"ffna{ti}"
            conv = (ti == 0)
            h_gp = h_s = 0
            for gp in range(8):
                sl1 = slots[(2 * gp + 2) % 4]
                sl2 = slots[(2 * gp + 3) % 4]
                for sl, fsrc, zsrc in ((sl1, hw1[gp], zw1[gp]),
                                       (sl2, hw2[gp], zw2[gp])):
                    wv_ = slot_dma(sl, fsrc if conv else zsrc, dst_bf=True)
                    sl["lastwv"] = wv_
                    if conv:
                        scratch_write(sl, zsrc, wv_)
                for j in range(4):
                    oc = gp * 4 + j
                    bA = pa_pool.acquire("tensor")
                    if j == 0:
                        em.wait("tensor", sl1["sem"], sl1["lastwv"])
                        em.wait("tensor", sl2["sem"], sl2["lastwv"])
                        if gp == 0:
                            em.wait("tensor", "v", ST["l1f_done"][ti])
                            if ti > 0:
                                em.wait("tensor", "st", 16 * NSUB * ti)
                    for k in range(8):
                        for vi, rhs in ((0, l1_hi), (1, l1_lo)):
                            fn = (lambda t, b=bA, k=k, j=j, sl=sl1, r=rhs, vi=vi:
                                  t.matmul(p[b][:, 0:256],
                                           sl["bf"][:, k * 512 + j * 128:k * 512 + (j + 1) * 128],
                                           r[:, k * 256:(k + 1) * 256],
                                           start=(k == 0 and vi == 0),
                                           stop=(k == 7 and vi == 1)))
                            mmA = em.op("tensor", fn,
                                        inc=("mm" if (k == 7 and vi == 1) else None))
                    bB = pa_pool.acquire("tensor")
                    for k in range(8):
                        for vi, rhs in ((0, l1_hi), (1, l1_lo)):
                            fn = (lambda t, b=bB, k=k, j=j, sl=sl2, r=rhs, vi=vi:
                                  t.matmul(p[b][:, 0:256],
                                           sl["bf"][:, k * 512 + j * 128:k * 512 + (j + 1) * 128],
                                           r[:, k * 256:(k + 1) * 256],
                                           start=(k == 0 and vi == 0),
                                           stop=(k == 7 and vi == 1)))
                            mmB = em.op("tensor", fn,
                                        inc=("mm" if (k == 7 and vi == 1) else None))
                    if j == 3:
                        slot_done(sl1, "mm", mmB)
                        slot_done(sl2, "mm", mmB)
                    fslot = fm_pool.acquire("scalar")
                    em.wait("scalar", "mm", mmA)
                    sA = fmst[:, fslot * 256:(fslot + 1) * 256]
                    scv = em.op("scalar", lambda sc, o=sA, b=bA, oc=oc:
                                sc.activation(o, p[b][:, 0:256], AF.Silu,
                                              scale=s1_sb[:, oc:oc + 1]),
                                inc="sc")
                    pa_pool.release(bA, "sc", scv)
                    fslot2 = fm_pool.acquire("vector")
                    em.wait("vector", "mm", mmB)
                    em.wait("vector", "sc", scv)
                    hout = fmst[:, fslot2 * 256:(fslot2 + 1) * 256]
                    vv = em.op("vector", lambda v, o=hout, b=bB, oc=oc, sA=sA:
                               v.scalar_tensor_tensor(
                                   o, p[b][:, 0:256], s2_sb[:, oc:oc + 1], sA,
                                   OP.mult, OP.mult), inc="v")
                    pa_pool.release(bB, "v", vv)
                    fm_pool.release(fslot, "v", vv)
                    em.wait("scalar", "v", vv)
                    hidst = h_hi[:, oc * 256:(oc + 1) * 256]
                    scv = em.op("scalar", lambda sc, d=hidst, s=hout:
                                sc.copy(d, s), inc="sc")
                    em.wait("vector", "sc", scv)
                    lodst = h_lo[:, oc * 256:(oc + 1) * 256]
                    vv = em.op("vector", lambda v, d=lodst, s=hout, hi=hidst:
                               v.tensor_sub(d, s, hi), inc="v")
                    fm_pool.release(fslot2, "v", vv)
                    h_gp, h_s = vv, scv
            ST["hsplit"] = (h_gp, h_s)

        def emit_ffn_b(ti):
            em.ctx = f"ffnb{ti}"
            conv = (ti == 0)
            h_gp, h_s = ST["hsplit"]

            def h_stage(fsrc_arr, zsrc_arr, scale_sb, in_hi, in_lo,
                        o_hi, o_lo, gate_gp, gate_s, mid_ti=None):
                last_gp = last_s = 0
                pend = []
                for oc in range(33):
                    if mid_ti is not None and oc == 1:
                        emit_midA(mid_ti, 0)
                        em.ctx = f"ffnb{ti}"
                    if mid_ti is not None and oc == 8:
                        emit_midA(mid_ti, 1)
                        em.ctx = f"ffnb{ti}"
                    if mid_ti is not None and oc == 12:
                        emit_midB(mid_ti)
                        em.ctx = f"ffnb{ti}"
                    if mid_ti is not None and oc == 13:
                        emit_midC(mid_ti, 0)
                        em.ctx = f"ffnb{ti}"
                    if mid_ti is not None and oc == 20:
                        emit_midC(mid_ti, 1)
                        em.ctx = f"ffnb{ti}"
                    if oc < 32:
                        sl = slots[oc % 4]
                        wv_ = slot_dma(sl, fsrc_arr[oc] if conv else zsrc_arr[oc],
                                       dst_bf=True)
                        if conv:
                            scratch_write(sl, zsrc_arr[oc], wv_)
                        bank = pa_pool.acquire("tensor")
                        em.wait("tensor", sl["sem"], wv_)
                        if oc == 0:
                            em.wait("tensor", "v", gate_gp)
                            em.wait("tensor", "sc", gate_s)
                        for k in range(32):
                            for vi, rhs in ((0, in_hi), (1, in_lo)):
                                fn = (lambda t, b=bank, k=k, sl=sl, r=rhs, vi=vi:
                                      t.matmul(p[b][:, 0:256],
                                               sl["bf"][:, k * 128:(k + 1) * 128],
                                               r[:, k * 256:(k + 1) * 256],
                                               start=(k == 0 and vi == 0),
                                               stop=(k == 31 and vi == 1)))
                                mmv = em.op("tensor", fn,
                                            inc=("mm" if (k == 31 and vi == 1)
                                                 else None))
                        slot_done(sl, "mm", mmv)
                        fslot = fm_pool.acquire("scalar")
                        em.wait("scalar", "mm", mmv)
                        gbuf = fmst[:, fslot * 256:(fslot + 1) * 256]
                        scv = em.op("scalar", lambda sc, o=gbuf, b=bank, oc=oc,
                                    ssb=scale_sb: sc.activation(
                                        o, p[b][:, 0:256], AF.Gelu,
                                        scale=ssb[:, oc:oc + 1]), inc="sc")
                        pa_pool.release(bank, "sc", scv)
                        hidst = o_hi[:, oc * 256:(oc + 1) * 256]
                        scv = em.op("scalar", lambda sc, d=hidst, s=gbuf:
                                    sc.copy(d, s), inc="sc")
                        lodst = o_lo[:, oc * 256:(oc + 1) * 256]
                        pend.append((scv, lodst, gbuf, hidst, fslot))
                        last_s = scv
                    while len(pend) > 0:
                        scv_, lodst_, gbuf_, hidst_, fs_ = pend.pop(0)
                        em.wait("vector", "sc", scv_)
                        gv = em.op("vector", lambda v, d=lodst_, s=gbuf_,
                                   hi=hidst_: v.tensor_sub(d, s, hi), inc="v")
                        fm_pool.release(fs_, "v", gv)
                        last_gp = gv
                return last_gp, last_s

            g1gp, g1s = h_stage(hh1, zh1, h1s_sb, h_hi, h_lo, g1_hi, g1_lo,
                                h_gp, h_s,
                                mid_ti=(ti + 1 if ti + 1 < ntiles else None))
            g2gp, g2s = h_stage(hh2, zh2, h2s_sb, g1_hi, g1_lo, h_hi, h_lo,
                                g1gp, g1s)

            if ti + 1 < ntiles:
                em.ctx = f"tailsplit{ti + 1}"
                em.wait("scalar", "v", ST["ctx_done"][ti + 1])
                hv_ = em.op("scalar", lambda sc: sc.copy(hi_tm, ctx_tm),
                            inc="sc")
                em.wait("vector", "sc", hv_)
                lv_ = em.op("vector", lambda v: v.tensor_sub(
                    lo_tm, ctx_tm, hi_tm), inc="v")
                ST["ctx_split"] = (hv_, lv_)
                ctxfm_done = 0
                for buf_i, (src_tm, dst_fm) in enumerate(
                        [(hi_tm, xf_hi), (lo_tm, xf_lo)]):
                    for cc in range(2):
                        bank = pt_pool.acquire("tensor")
                        if buf_i == 0 and cc == 0:
                            em.wait("tensor", "sc", hv_)
                        if buf_i == 1 and cc == 0:
                            em.wait("tensor", "v", lv_)
                        for q in range(4):
                            c = cc * 4 + q
                            for s_i in range(NSUB):
                                fn = (lambda t, b=bank, q=q, s=s_i, c=c,
                                      src=src_tm: t.transpose(
                                          p[b][:].bitcast(BF16)[
                                              :, (q * 2 + s) * 128:(q * 2 + s + 1) * 128],
                                          src.rearrange("q (s f) -> q s f", f=D)[
                                              :, s, c * 128:(c + 1) * 128],
                                          idb_sb[:]))
                                mmv = em.op("tensor", fn,
                                            inc=("mm" if (q == 3 and s_i == 1)
                                                 else None))
                        em.wait("vector", "mm", mmv)
                        dst = dst_fm.rearrange("q (c t) -> q c t", t=256)[
                            :, cc * 4:cc * 4 + 4, :]
                        src = p[bank][:].bitcast(BF16)[:, 0:1024] \
                            .rearrange("q (c t) -> q c t", t=256)
                        vv_ = em.op("vector", lambda v, d=dst, s=src:
                                    v.tensor_copy(d, s), inc="v")
                        pt_pool.release(bank, "v", vv_)
                        ctxfm_done = vv_
                ST["ctxfm_done"] = ctxfm_done
                em.ctx = f"ffnb{ti}"
            for it in pt_pool.items:
                fa = pt_pool.free_at[it]
                if fa:
                    em.wait("tensor", *fa)
            em.wait("tensor", "v", g2gp)
            em.wait("tensor", "sc", g2s)
            if ST.get("w3_evac"):
                em.wait("tensor", "v", ST["w3_evac"])
            for gw in range(8):
                sl = slots[gw % 2]
                wv_ = slot_dma(sl, hw3[gw] if conv else zw3[gw], dst_bf=True)
                if conv:
                    scratch_write(sl, zw3[gw], wv_)
                em.wait("tensor", sl["sem"], wv_)
                for kk in range(4):
                    k = gw * 4 + kk
                    for src in (h_hi, h_lo):
                        for s_i in range(NSUB):
                            for nh in range(2):
                                fn = (lambda t, k=k, kk=kk, s=s_i, nh=nh,
                                      sl=sl, src=src: t.matmul(
                                          p[2 + 2 * s + nh][:],
                                          src[:, k * 256 + s * 128:k * 256 + (s + 1) * 128],
                                          sl["bf"][:, kk * 1024 + nh * 512:kk * 1024 + (nh + 1) * 512],
                                          start=(k == 0 and src is h_hi),
                                          stop=(k == 31 and src is h_lo)))
                                mmv = em.op("tensor", fn,
                                            inc=("mm" if (kk == 3 and
                                                          src is h_lo and
                                                          s_i == 1 and nh == 1)
                                                 else None))
                slot_done(sl, "mm", em.val("mm"))
            w3mm = em.val("mm")
            em.wait("vector", "mm", w3mm)
            for s_i in range(NSUB):
                for nh in range(2):
                    dst = y2[:, s_i * D + nh * 512:s_i * D + (nh + 1) * 512]
                    lsrc = ln1[:, s_i * D + nh * 512:s_i * D + (nh + 1) * 512]
                    em.op("vector", lambda v, d=dst, s=s_i, nh=nh:
                          v.tensor_tensor(d, p[2 + 2 * s + nh][:],
                                          s3_sb[:, nh * 512:(nh + 1) * 512],
                                          OP.mult))
                    vv = em.op("vector", lambda v, d=dst, ls=lsrc:
                               v.tensor_tensor(d, d, ls, OP.add), inc="v")
            w3_evac = vv
            ST["w3_evac"] = w3_evac
            for b_ in (2, 3, 4, 5):
                pa_pool.free_at[b_] = ("v", w3_evac)
            y2v = y2.rearrange("q (s f) -> q s f", f=D)
            em.op("vector", lambda v: v.reduce_sum(lnb[:, 0:2], y2v, axis=AX))
            em.wait("scalar", "v", w3_evac)
            scv = em.op("scalar", lambda sc: sc.activation(
                sq2, y2, AF.Square), inc="sc")
            em.wait("vector", "sc", scv)
            em.op("vector", lambda v: v.reduce_sum(
                lnb[:, 4:6], sq2.rearrange("q (s f) -> q s f", f=D), axis=AX))
            em.op("vector", lambda v: v.tensor_scalar_mul(
                lnb[:, 8:10], lnb[:, 0:2], 1.0 / D))
            vpad()
            em.op("vector", lambda v: v.tensor_mul(
                lnb[:, 12:14], lnb[:, 8:10], lnb[:, 8:10]))
            vpad()
            em.op("vector", lambda v: v.scalar_tensor_tensor(
                lnb[:, 16:18], lnb[:, 4:6], 1.0 / D, lnb[:, 12:14],
                OP.mult, OP.subtract))
            vpad()
            vv = em.op("vector", lambda v: v.tensor_scalar_add(
                lnb[:, 20:22], lnb[:, 16:18], 1e-5), inc="v")
            em.wait("scalar", "v", vv)
            scv = em.op("scalar", lambda sc: sc.activation(
                lnb[:, 24:26], lnb[:, 20:22], AF.Sqrt), inc="sc")
            em.wait("vector", "sc", scv)
            em.op("vector", lambda v: v.reciprocal(lnb[:, 28:30], lnb[:, 24:26]))
            vpad()
            em.op("vector", lambda v: v.tensor_mul(
                lnb[:, 32:34], lnb[:, 28:30], lnb[:, 28:30]))
            vpad()
            em.op("vector", lambda v: v.tensor_mul(
                lnb[:, 36:38], lnb[:, 20:22], lnb[:, 32:34]))
            vpad()
            em.op("vector", lambda v: v.tensor_scalar(
                lnb[:, 40:42], lnb[:, 36:38], -0.5, 1.5, OP.mult, OP.add))
            vpad()
            em.op("vector", lambda v: v.tensor_mul(
                lnb[:, 44:46], lnb[:, 28:30], lnb[:, 40:42]))
            vpad()
            for s_i in range(NSUB):
                rb = lnb[:, 44 + s_i:45 + s_i].broadcast_to((128, D))
                fn = (lambda v, s=s_i, rb=rb: v.scalar_tensor_tensor(
                    out_tm[:, s * D:(s + 1) * D], y2v[:, s, :],
                    lnb[:, 8 + s:9 + s], rb, OP.subtract, OP.mult))
                vv = em.op("vector", fn, inc="v")
            ST["out_done"] = vv
            em.wait("scalar", "v", vv)
            for s_i in range(NSUB):
                em.op("scalar", lambda sc, s=s_i, t=ti: sc.dma_start(
                    out=ho[t, s], in_=out_tm[:, s * D:(s + 1) * D]),
                    inc="st", k=16)

        emit_head(0)
        emit_mid(0)
        emit_tail(0)
        for i in range(ntiles):
            if i + 1 < ntiles:
                emit_head(i + 1)
            emit_ffn_a(i)
            emit_ffn_b(i)
            if i + 1 < ntiles:
                emit_tail(i + 1)
        em.wait("scalar", "st", em.val("st"))

        def replay(eng_name):
            def run(e):
                for item in em.ops[eng_name]:
                    if item[0] == "w":
                        e.wait_ge(sems[item[1]], item[2])
                    else:
                        _, fn, incn, k = item[:4]
                        instr = fn(e)
                        if incn is not None:
                            instr.then_inc(sems[incn], k)
            return run

        blk.sync(replay("sync"))
        blk.gpsimd(replay("gpsimd"))
        blk.tensor(replay("tensor"))
        blk.vector(replay("vector"))
        blk.scalar(replay("scalar"))

    globals()['LAST_EM'] = em
    return nc


def _prep_weights(inputs):
    f32 = np.float32

    def c(a):
        return np.ascontiguousarray(a, dtype=f32)

    Wq = np.asarray(inputs["L0_attn_W_qkv"], dtype=f32)
    Wo = np.asarray(inputs["L0_attn_W_out"], dtype=f32)
    w1T = np.asarray(inputs["L0_ffn_w1_w"], dtype=f32).T
    w2T = np.asarray(inputs["L0_ffn_w2_w"], dtype=f32).T
    h1T = np.asarray(inputs["L0_ffn_h1_w"], dtype=f32).T
    h2T = np.asarray(inputs["L0_ffn_h2_w"], dtype=f32).T
    w3T = np.asarray(inputs["L0_ffn_w3_w"], dtype=f32).T
    d = {
        "wq": c(Wq.reshape(8, 128, 12, 256).transpose(2, 1, 0, 3)
                .reshape(12, 128, 2048)),
        "wo": c(Wo.reshape(8, 128, 4, 256).transpose(2, 1, 0, 3)
                .reshape(4, 128, 2048)),
        "w1": c(w1T.reshape(8, 128, 8, 512).transpose(2, 1, 0, 3)
                .reshape(8, 128, 4096)),
        "w2": c(w2T.reshape(8, 128, 8, 512).transpose(2, 1, 0, 3)
                .reshape(8, 128, 4096)),
        "h1": c(h1T.reshape(32, 128, 32, 128).transpose(2, 1, 0, 3)
                .reshape(32, 128, 4096)),
        "h2": c(h2T.reshape(32, 128, 32, 128).transpose(2, 1, 0, 3)
                .reshape(32, 128, 4096)),
        "w3": c(w3T.reshape(8, 4, 128, 1024).transpose(0, 2, 1, 3)
                .reshape(8, 128, 4096)),
        "bqbc": c(np.broadcast_to(
            np.asarray(inputs["L0_attn_b_qkv"], dtype=f32), (128, QKVD))),
        "bobc": c(np.broadcast_to(
            np.asarray(inputs["L0_attn_b_out"], dtype=f32), (128, D))),
        "s1": c(np.asarray(inputs["L0_ffn_w1_s"], dtype=f32)
                .reshape(32, 128).T),
        "s2": c(np.asarray(inputs["L0_ffn_w2_s"], dtype=f32)
                .reshape(32, 128).T),
        "h1s": c(np.asarray(inputs["L0_ffn_h1_s"], dtype=f32)
                 .reshape(32, 128).T),
        "h2s": c(np.asarray(inputs["L0_ffn_h2_s"], dtype=f32)
                 .reshape(32, 128).T),
        "s3bc": c(np.broadcast_to(
            np.asarray(inputs["L0_ffn_w3_s"], dtype=f32), (128, D))),
        "idf": np.eye(128, dtype=f32),
        "idb": np.eye(128, dtype=ml_dtypes.bfloat16),
    }
    return d


def kernel(**inputs):
    x = np.ascontiguousarray(
        np.asarray(inputs["L0_attn_x"], dtype=np.float32).reshape(-1, D))
    ntok = x.shape[0]
    per = ntok // NCORES
    wd = _prep_weights(inputs)
    nc = build_nc(NTILES)
    in_maps = []
    for ci in range(NCORES):
        m = dict(wd)
        m["x"] = np.ascontiguousarray(
            x[ci * per:(ci + 1) * per].reshape(NTILES, NSUB, 128, D))
        in_maps.append(m)
    res = run_bass_kernel_spmd(nc, in_maps, core_ids=list(range(NCORES)))
    out = np.concatenate(
        [np.asarray(r["out"]).reshape(per, D) for r in res.results], axis=0)
    return out.reshape(np.asarray(inputs["L0_attn_x"]).shape)


# version 17
# speedup vs baseline: 1.1400x; 1.0726x over previous
import numpy as np
import ml_dtypes
import concourse.bass as bass
import concourse.mybir as mybir
from concourse.bass_utils import run_bass_kernel_spmd
from contextlib import ExitStack

F32 = mybir.dt.float32
BF16 = mybir.dt.bfloat16
AF = mybir.ActivationFunctionType
OP = mybir.AluOpType
AX = mybir.AxisListType.X

NCORES = 8
D = 1024
QKVD = 3072
DH = 4096
T = 256
NSUB = 2
NTILES = 16
ENGS = ("sync", "gpsimd", "tensor", "vector", "scalar")


class Em:
    def __init__(self):
        self.ops = {e: [] for e in ENGS}
        self.cnt = {}
        self.ctx = ""

    def wait(self, eng, sem, val):
        if val > 0:
            self.ops[eng].append(("w", sem, val, self.ctx))

    def op(self, eng, fn, inc=None, k=1):
        if inc is not None:
            self.cnt[inc] = self.cnt.get(inc, 0) + k
        self.ops[eng].append(("o", fn, inc, k, self.ctx))
        return self.cnt.get(inc, 0)

    def val(self, sem):
        return self.cnt.get(sem, 0)


class Pool:
    def __init__(self, em, items):
        self.em = em
        self.items = list(items)
        self.free_at = {i: None for i in items}
        self.rr = 0

    def acquire(self, eng):
        it = self.items[self.rr % len(self.items)]
        self.rr += 1
        fa = self.free_at[it]
        if fa is not None:
            self.em.wait(eng, fa[0], fa[1])
        return it

    def release(self, it, sem, val):
        self.free_at[it] = (sem, val)


def build_nc(ntiles=NTILES):
    nc = bass.Bass("TRN2", target_bir_lowering=False, debug=False)
    hx = nc.dram_tensor("x", [ntiles, NSUB, 128, D], F32, kind="ExternalInput").ap()
    ho = nc.dram_tensor("out", [ntiles, NSUB, 128, D], F32, kind="ExternalOutput").ap()
    hwq = nc.dram_tensor("wq", [12, 128, 2048], F32, kind="ExternalInput").ap()
    hwo = nc.dram_tensor("wo", [4, 128, 2048], F32, kind="ExternalInput").ap()
    hw1 = nc.dram_tensor("w1", [8, 128, 4096], F32, kind="ExternalInput").ap()
    hw2 = nc.dram_tensor("w2", [8, 128, 4096], F32, kind="ExternalInput").ap()
    hh1 = nc.dram_tensor("h1", [32, 128, 4096], F32, kind="ExternalInput").ap()
    hh2 = nc.dram_tensor("h2", [32, 128, 4096], F32, kind="ExternalInput").ap()
    hw3 = nc.dram_tensor("w3", [8, 128, 4096], F32, kind="ExternalInput").ap()
    hbq = nc.dram_tensor("bqbc", [128, QKVD], F32, kind="ExternalInput").ap()
    hbo = nc.dram_tensor("bobc", [128, D], F32, kind="ExternalInput").ap()
    hs1 = nc.dram_tensor("s1", [128, 32], F32, kind="ExternalInput").ap()
    hs2 = nc.dram_tensor("s2", [128, 32], F32, kind="ExternalInput").ap()
    hh1s = nc.dram_tensor("h1s", [128, 32], F32, kind="ExternalInput").ap()
    hh2s = nc.dram_tensor("h2s", [128, 32], F32, kind="ExternalInput").ap()
    hs3 = nc.dram_tensor("s3bc", [128, D], F32, kind="ExternalInput").ap()
    hidf = nc.dram_tensor("idf", [128, 128], F32, kind="ExternalInput").ap()
    hidb = nc.dram_tensor("idb", [128, 128], BF16, kind="ExternalInput").ap()
    zw1 = nc.dram_tensor("zw1", [8, 128, 4096], BF16, kind="Internal").ap()
    zw2 = nc.dram_tensor("zw2", [8, 128, 4096], BF16, kind="Internal").ap()
    zh1 = nc.dram_tensor("zh1", [32, 128, 4096], BF16, kind="Internal").ap()
    zh2 = nc.dram_tensor("zh2", [32, 128, 4096], BF16, kind="Internal").ap()
    zw3 = nc.dram_tensor("zw3", [8, 128, 4096], BF16, kind="Internal").ap()
    zwq = nc.dram_tensor("zwq", [12, 128, 4096], BF16, kind="Internal").ap()
    zwo = nc.dram_tensor("zwo", [4, 128, 4096], BF16, kind="Internal").ap()

    with ExitStack() as st:
        def sb(name, shape, dt=F32):
            return st.enter_context(nc.sbuf_tensor(name, shape, dt))

        regA = sb("regA", [128, 8192])
        regB = sb("regB", [128, 8192])
        qkvt = sb("qkvt", [128, 6144])
        patt = sb("patt", [128, 2048])
        ctxa = sb("ctxa", [128, 3392])
        x_t = [sb("x_t0", [128, 2048]), sb("x_t1", [128, 2048])]
        xfm = sb("xfm", [128, 2048])
        ring = sb("ring", [128, 8192])
        ln1 = sb("ln1", [128, 2048])
        l1f = sb("l1f", [128, 2048])
        fmst = sb("fmst", [128, 2048])
        lnb = sb("lnb", [128, 64])
        s1_sb = sb("s1_sb", [128, 32])
        s2_sb = sb("s2_sb", [128, 32])
        h1s_sb = sb("h1s_sb", [128, 32])
        h2s_sb = sb("h2s_sb", [128, 32])
        s3_sb = sb("s3_sb", [128, D])
        idf_sb = sb("idf_sb", [128, 128])
        idb_sb = sb("idb_sb", [128, 128], BF16)
        p = [st.enter_context(nc.psum_tensor(f"p{i}", [128, 512], F32))
             for i in range(8)]

        sem_names = ["cst", "x", "st", "mm", "v", "sc", "bs", "sw", "gp",
                     "sl0", "sl1", "sl2", "sl3"]
        sems = {n: st.enter_context(nc.semaphore(n)) for n in sem_names}
        blk = st.enter_context(nc.Block())

        A = regA[:]
        B = regB[:]
        Ab = A.bitcast(BF16)
        Bb = B.bitcast(BF16)
        pb = patt[:].bitcast(BF16)

        h_hi = Ab[:, 0:8192]
        h_lo = Ab[:, 8192:16384]
        g1_hi = Bb[:, 0:8192]
        g1_lo = Bb[:, 8192:16384]
        out_tm = B[:, 0:2048]
        sq2 = B[:, 2048:4096]
        y2 = B[:, 4096:6144]
        bobc = B[:, 2048:3072]
        bqbc = B[:, 3072:6144]
        Pscr = patt[:]
        hi_tm = pb[:, 0:2048]
        lo_tm = pb[:, 2048:4096]
        ctx_tm = ctxa[:, 0:2048]
        S_s = [ctxa[:, 2048:2304], ctxa[:, 2304:2560]]
        E_s = [ctxa[:, 2560:2816], ctxa[:, 2816:3072]]
        A_b = ctxa[:, 3072:3328]
        M_s = [ctxa[:, 3328:3344], ctxa[:, 3344:3360]]
        R_b = ctxa[:, 3360:3376]
        Rr_b = ctxa[:, 3376:3392]
        l1_hi = l1f[:].bitcast(BF16)[:, 0:2048]
        l1_lo = l1f[:].bitcast(BF16)[:, 2048:4096]
        xf_hi = xfm[:].bitcast(BF16)[:, 0:2048]
        xf_lo = xfm[:].bitcast(BF16)[:, 2048:4096]


        slots = []
        for i in range(4):
            f32v = ring[:, i * 2048:(i + 1) * 2048]
            bfv = ring[:].bitcast(BF16)[:, i * 4096:(i + 1) * 4096]
            slots.append({"f32": f32v, "bf": bfv, "sem": f"sl{i}", "n": 0,
                          "idx": i})

        em = Em()

        def vpad(n=8):
            for _ in range(n):
                em.op("vector", lambda v: v.engine_nop())

        pa_pool = Pool(em, [0, 1, 2, 3, 4, 5])
        pt_pool = Pool(em, [6, 7])
        fm_pool = Pool(em, list(range(8)))
        slot_free = {i: [] for i in range(4)}

        def slot_dma(slot, src_ap, dst_bf, eng="gpsimd"):
            i = slot["idx"]
            for w_ in slot_free[i]:
                em.wait(eng, *w_)
            slot_free[i] = []
            dst = slot["bf"] if dst_bf else slot["f32"]
            em.op(eng, lambda g, d=dst, s=src_ap: g.dma_start(out=d, in_=s),
                  inc=slot["sem"], k=16)
            slot["n"] += 1
            return 16 * slot["n"]

        def slot_done(slot, sem, val):
            slot_free[slot["idx"]].append((sem, val))

        def scratch_write(slot, dst_ap, wv_):
            em.wait("scalar", slot["sem"], wv_)
            swv = em.op("scalar", lambda sc, d=dst_ap, s=slot["bf"]:
                        sc.dma_start(out=d, in_=s), inc="sw", k=16)
            slot_done(slot, "sw", swv)

        stg_free = {2: 0, 3: 0}

        def split_write(slot, wv_, dst_ap, si):
            stg = slots[2 + (si & 1)]
            em.wait("scalar", slot["sem"], wv_)
            if stg_free[stg["idx"]]:
                em.wait("scalar", "sw", stg_free[stg["idx"]])
            hv = em.op("scalar", lambda sc, d=stg["bf"], s=slot["f32"]:
                       sc.copy(d[:, 0:2048], s), inc="sc")
            em.wait("vector", "sc", hv)
            lv = em.op("vector", lambda v, d=stg["bf"], s=slot["f32"]:
                       v.tensor_sub(d[:, 2048:4096], s, d[:, 0:2048]), inc="v")
            em.wait("scalar", "v", lv)
            swv = em.op("scalar", lambda sc, d=dst_ap, s=stg["bf"]:
                        sc.dma_start(out=d, in_=s), inc="sw", k=16)
            stg_free[stg["idx"]] = swv
            slot_free[stg["idx"]].append(("sw", swv))
            slot_free[slot["idx"]].append(("v", lv))

        consts = [(s1_sb[:], hs1), (s2_sb[:], hs2),
                  (h1s_sb[:], hh1s), (h2s_sb[:], hh2s), (s3_sb[:], hs3),
                  (idf_sb[:], hidf), (idb_sb[:], hidb)]
        for d_, s_ in consts:
            em.op("sync", lambda sy, d=d_, s=s_: sy.dma_start(out=d, in_=s),
                  inc="cst", k=16)
        cst_all = em.val("cst")
        for t_ in range(min(2, ntiles)):
            for s_ in range(NSUB):
                em.op("sync", lambda sy, t=t_, s=s_: sy.dma_start(
                    out=x_t[t][:, s * D:(s + 1) * D], in_=hx[t, s]),
                    inc="x", k=16)

        ST = {
            "ln1_apply": [0, 0],
            "outproj_mm": 0,
            "ln1tp_mm": 0,
            "out_done": 0,
            "ctx_done": {},
            "l1f_done": {},
            "split_done": {},
        }

        xv2 = [x_t[0][:].rearrange("q (s f) -> q s f", f=D),
               x_t[1][:].rearrange("q (s f) -> q s f", f=D)]
        qkvv = qkvt[:].rearrange("q (s f) -> q s f", f=QKVD)

        def emit_head(ti):
            em.ctx = f"head{ti}"
            xb = ti % 2
            if ti == 0:
                em.wait("tensor", "cst", cst_all)
                em.wait("vector", "cst", cst_all)
                em.wait("scalar", "cst", cst_all)
            if ST["out_done"]:
                em.wait("sync", "v", ST["out_done"])
            em.wait("sync", "mm", em.val("mm"))
            em.op("sync", lambda sy: sy.dma_start(out=bobc, in_=hbo),
                  inc="bs", k=16)
            em.op("sync", lambda sy: sy.dma_start(out=bqbc, in_=hbq),
                  inc="bs", k=16)
            bias_v = em.val("bs")
            em.wait("tensor", "x", 16 * NSUB * (ti + 1))
            xe = 0
            if ti == 0:
                for grp in range(4):
                    s_i, cc = grp // 2, grp % 2
                    bank = pt_pool.acquire("tensor")
                    for q in range(4):
                        c = cc * 4 + q
                        fn = (lambda t, b=bank, q=q, s=s_i, c=c, xb=xb:
                              t.transpose(p[b][:, q * 128:(q + 1) * 128],
                                          x_t[xb][:, s * D + c * 128:s * D + (c + 1) * 128],
                                          idf_sb[:]))
                        mmv = em.op("tensor", fn, inc=("mm" if q == 3 else None))
                    em.wait("vector", "mm", mmv)
                    dst = xfm[:].rearrange("q (c t) -> q c t", t=256)[
                        :, cc * 4:cc * 4 + 4, s_i * 128:(s_i + 1) * 128]
                    src = p[bank][:, 0:512].rearrange("q (c t) -> q c t", t=128)
                    vv = em.op("vector", lambda v, d=dst, s=src:
                               v.tensor_copy(d, s), inc="v")
                    pt_pool.release(bank, "v", vv)
                    xe = vv
            else:
                if ST["ln1tp_mm"]:
                    em.wait("scalar", "mm", ST["ln1tp_mm"])
                hv = em.op("scalar", lambda sc, xb=xb:
                           sc.copy(hi_tm, x_t[xb][:]), inc="sc")
                em.wait("vector", "sc", hv)
                lv = em.op("vector", lambda v, xb=xb: v.tensor_sub(
                    lo_tm, x_t[xb][:], hi_tm), inc="v")
                for buf_i, (src_tm, dst_fm) in enumerate(
                        [(hi_tm, xf_hi), (lo_tm, xf_lo)]):
                    for cc in range(2):
                        bank = pt_pool.acquire("tensor")
                        if buf_i == 0 and cc == 0:
                            em.wait("tensor", "sc", hv)
                        if buf_i == 1 and cc == 0:
                            em.wait("tensor", "v", lv)
                        for q in range(4):
                            c = cc * 4 + q
                            for s_i in range(NSUB):
                                fn = (lambda t, b=bank, q=q, s=s_i, c=c,
                                      src=src_tm: t.transpose(
                                          p[b][:].bitcast(BF16)[
                                              :, (q * 2 + s) * 128:(q * 2 + s + 1) * 128],
                                          src.rearrange("q (s f) -> q s f", f=D)[
                                              :, s, c * 128:(c + 1) * 128],
                                          idb_sb[:]))
                                mmv = em.op("tensor", fn,
                                            inc=("mm" if (q == 3 and s_i == 1)
                                                 else None))
                        em.wait("vector", "mm", mmv)
                        if buf_i == 0 and cc == 0 and ST["outproj_mm"]:
                            em.wait("vector", "mm", ST["outproj_mm"])
                        dst = dst_fm.rearrange("q (c t) -> q c t", t=256)[
                            :, cc * 4:cc * 4 + 4, :]
                        src = p[bank][:].bitcast(BF16)[:, 0:1024] \
                            .rearrange("q (c t) -> q c t", t=256)
                        vv = em.op("vector", lambda v, d=dst, s=src:
                                   v.tensor_copy(d, s), inc="v")
                        pt_pool.release(bank, "v", vv)
                        xe = vv
            ST["xtp_mm"] = mmv
            em.wait("vector", "bs", bias_v)
            for s_i in range(NSUB):
                fn_ = (lambda v, s=s_i, xb=xb: v.tensor_tensor(
                    x_t[xb][:, s * D:(s + 1) * D],
                    x_t[xb][:, s * D:(s + 1) * D], bobc[:], OP.add))
                em.op("vector", fn_, inc="v")
            for g in range(12):
                slot = slots[g % 2]
                if ti == 0:
                    wv_ = slot_dma(slot, hwq[g], dst_bf=False, eng="sync")
                    split_write(slot, wv_, zwq[g], g)
                else:
                    wv_ = slot_dma(slot, zwq[g], dst_bf=True)
                last_mm = 0
                for s_i in range(NSUB):
                    bank = pa_pool.acquire("tensor")
                    if s_i == 0:
                        em.wait("tensor", slot["sem"], wv_)
                        em.wait("tensor", "v", xe)
                    if ti == 0:
                        for k in range(8):
                            fn = (lambda t, b=bank, k=k, s=s_i, sl=slot:
                                  t.matmul(p[b][:, 0:256],
                                           xfm[:, k * 256 + s * 128:k * 256 + (s + 1) * 128],
                                           sl["f32"][:, k * 256:(k + 1) * 256],
                                           start=(k == 0), stop=(k == 7)))
                            mmv = em.op("tensor", fn,
                                        inc=("mm" if k == 7 else None))
                    else:
                        passes = [(xf_hi, 0), (xf_hi, 2048), (xf_lo, 0)]
                        for pi, (xbuf, roff) in enumerate(passes):
                            for k in range(8):
                                fn = (lambda t, b=bank, k=k, s=s_i, sl=slot,
                                      xbuf=xbuf, roff=roff, pi=pi: t.matmul(
                                          p[b][:, 0:256],
                                          xbuf[:, k * 256 + s * 128:k * 256 + (s + 1) * 128],
                                          sl["bf"][:, roff + k * 256:roff + (k + 1) * 256],
                                          start=(pi == 0 and k == 0),
                                          stop=(pi == 2 and k == 7)))
                                mmv = em.op("tensor", fn,
                                            inc=("mm" if (pi == 2 and k == 7)
                                                 else None))
                    last_mm = mmv
                    em.wait("vector", "mm", mmv)
                    dst = qkvv[:, s_i, g * 256:(g + 1) * 256]
                    vv = em.op("vector", lambda v, d=dst, b=bank, g=g:
                               v.tensor_tensor(d, p[b][:, 0:256],
                                               bqbc[:, g * 256:(g + 1) * 256],
                                               OP.add), inc="v")
                    pa_pool.release(bank, "v", vv)
                slot_done(slot, "mm", last_mm)

        def emit_midA(ti, s_i):
            em.ctx = f"mid{ti}"
            if s_i == 0 and ST.get("xtp_mm"):
                em.wait("vector", "mm", ST["xtp_mm"])
            Pv = Pscr.rearrange("q (h g d) -> q h g d", g=16, d=64)
            S_b, M_b = S_s[s_i], M_s[s_i]
            qk = qkvv[:, s_i, :].rearrange("q (h j) -> q h j", j=192)
            for hf in range(8):
                qa = qk[:, 2 * hf:2 * (hf + 1), 0:64].unsqueeze(2) \
                    .broadcast_to((128, 2, 16, 64))
                ka = qk[:, 0:16, 64:128].unsqueeze(1) \
                    .broadcast_to((128, 2, 16, 64))
                em.op("vector", lambda v, qa=qa, ka=ka:
                      v.tensor_tensor(Pv[:, 0:2], qa, ka, OP.mult))
                sdst = S_b[:, 32 * hf:32 * (hf + 1)] \
                    .rearrange("q (h g) -> q h g", g=16)
                em.op("vector", lambda v, d=sdst:
                      v.reduce_sum(d, Pv[:, 0:2], axis=AX))
            Sv = S_b.rearrange("q (h g) -> q h g", g=16)
            em.op("vector", lambda v, S=Sv, M=M_b: v.reduce_max(M, S, axis=AX))
            vpad(4)
            mbc = M_b.unsqueeze(2).broadcast_to((128, 16, 16))
            zv = em.op("vector", lambda v, S=Sv, m=mbc:
                       v.tensor_tensor(S, S, m, OP.subtract), inc="v")
            ST[f"zv{s_i}"] = zv

        def emit_midB(ti):
            em.ctx = f"mid{ti}"
            for s_i in range(NSUB):
                em.wait("scalar", "v", ST[f"zv{s_i}"])
                ev = em.op("scalar", lambda sc, s=s_i: sc.activation(
                    E_s[s], S_s[s], AF.Exp, scale=0.125), inc="sc")
                ST[f"ev{s_i}"] = ev

        def emit_midC(ti, s_i):
            em.ctx = f"mid{ti}"
            P2v = Pscr.rearrange("q (h d g) -> q h d g", d=64, g=16)
            E_b = E_s[s_i]
            qk = qkvv[:, s_i, :].rearrange("q (h j) -> q h j", j=192)
            em.wait("vector", "sc", ST[f"ev{s_i}"])
            Ev = E_b.rearrange("q (h g) -> q h g", g=16)
            em.op("vector", lambda v, E=Ev: v.reduce_sum(R_b, E, axis=AX))
            vpad(6)
            em.op("vector", lambda v: v.reciprocal(Rr_b, R_b))
            vpad(6)
            rbc = Rr_b.unsqueeze(2).broadcast_to((128, 16, 16))
            em.op("vector", lambda v, r=rbc, E=Ev: v.tensor_tensor(
                A_b.rearrange("q (h g) -> q h g", g=16), E, r, OP.mult))
            last = 0
            for hf in range(8):
                Aex = A_b.rearrange("q (h g) -> q h g", g=16)[
                    :, 2 * hf:2 * (hf + 1), :].unsqueeze(2) \
                    .broadcast_to((128, 2, 64, 16))
                Vex = qk[:, 0:16, 128:192].transpose([0, 2, 1]) \
                    .unsqueeze(1).broadcast_to((128, 2, 64, 16))
                em.op("vector", lambda v, a=Aex, w=Vex:
                      v.tensor_tensor(P2v[:, 0:2], a, w, OP.mult))
                cdst = ctx_tm.rearrange("q (s f) -> q s f", f=D)[
                    :, s_i, 128 * hf:128 * (hf + 1)] \
                    .rearrange("q (h d) -> q h d", d=64)
                last = em.op("vector", lambda v, d=cdst:
                             v.reduce_sum(d, P2v[:, 0:2], axis=AX), inc="v")
            if s_i == NSUB - 1:
                ST["ctx_done"][ti] = last

        def emit_mid(ti):
            emit_midA(ti, 0)
            emit_midA(ti, 1)
            emit_midB(ti)
            emit_midC(ti, 0)
            emit_midC(ti, 1)

        def emit_tail(ti):
            em.ctx = f"tail{ti}"
            xb = ti % 2
            ctxfm_done = 0
            if ti == 0:
                for s_i in range(NSUB):
                    for cc in range(2):
                        bank = pt_pool.acquire("tensor")
                        em.wait("tensor", "v", ST["ctx_done"][ti])
                        for q in range(4):
                            c = cc * 4 + q
                            fn = (lambda t, b=bank, q=q, s=s_i, c=c: t.transpose(
                                p[b][:, q * 128:(q + 1) * 128],
                                ctx_tm.rearrange("q (s f) -> q s f", f=D)[
                                    :, s, c * 128:(c + 1) * 128],
                                idf_sb[:]))
                            mmv = em.op("tensor", fn,
                                        inc=("mm" if q == 3 else None))
                        em.wait("vector", "mm", mmv)
                        dst = xfm[:].rearrange("q (c t) -> q c t", t=256)[
                            :, cc * 4:cc * 4 + 4, s_i * 128:(s_i + 1) * 128]
                        src = p[bank][:, 0:512].rearrange("q (c t) -> q c t",
                                                          t=128)
                        vv = em.op("vector", lambda v, d=dst, s=src:
                                   v.tensor_copy(d, s), inc="v")
                        pt_pool.release(bank, "v", vv)
                        ctxfm_done = vv
            else:
                ctxfm_done = ST["ctxfm_done"]
            if ti == 0:
                for g in range(4):
                    slot = slots[g % 2]
                    wv_ = slot_dma(slot, hwo[g], dst_bf=False, eng="sync")
                    split_write(slot, wv_, zwo[g], g)
                    last_mm = 0
                    for s_i in range(NSUB):
                        bank = pa_pool.acquire("tensor")
                        if s_i == 0:
                            em.wait("tensor", slot["sem"], wv_)
                            em.wait("tensor", "v", ctxfm_done)
                        for k in range(8):
                            fn = (lambda t, b=bank, k=k, s=s_i, sl=slot:
                                  t.matmul(p[b][:, 0:256],
                                           xfm[:, k * 256 + s * 128:k * 256 + (s + 1) * 128],
                                           sl["f32"][:, k * 256:(k + 1) * 256],
                                           start=(k == 0), stop=(k == 7)))
                            mmv = em.op("tensor", fn,
                                        inc=("mm" if k == 7 else None))
                        last_mm = mmv
                        em.wait("vector", "mm", mmv)
                        dst = xv2[xb][:, s_i, g * 256:(g + 1) * 256]
                        vv = em.op("vector", lambda v, d=dst, b=bank:
                                   v.tensor_tensor(d, p[b][:, 0:256], d,
                                                   OP.add), inc="v")
                        pa_pool.release(bank, "v", vv)
                    slot_done(slot, "mm", last_mm)
                resid_s = [0, 0]
                resid_s[NSUB - 1] = vv
                resid_done = vv
            else:
                wo_wv = {}
                for g in range(4):
                    wo_wv[g] = slot_dma(slots[g], zwo[g], dst_bf=True)
                resid_s = [0, 0]
                last_mm = 0
                for s_i in range(NSUB):
                    for g in range(4):
                        slot = slots[g]
                        bank = pa_pool.acquire("tensor")
                        if s_i == 0:
                            em.wait("tensor", slot["sem"], wo_wv[g])
                            if g == 0:
                                em.wait("tensor", "v", ctxfm_done)
                        passes = [(xf_hi, 0), (xf_hi, 2048), (xf_lo, 0)]
                        for pi, (xbuf, roff) in enumerate(passes):
                            for k in range(8):
                                fn = (lambda t, b=bank, k=k, s=s_i, sl=slot,
                                      xbuf=xbuf, roff=roff, pi=pi: t.matmul(
                                          p[b][:, 0:256],
                                          xbuf[:, k * 256 + s * 128:k * 256 + (s + 1) * 128],
                                          sl["bf"][:, roff + k * 256:roff + (k + 1) * 256],
                                          start=(pi == 0 and k == 0),
                                          stop=(pi == 2 and k == 7)))
                                mmv = em.op("tensor", fn,
                                            inc=("mm" if (pi == 2 and k == 7)
                                                 else None))
                        last_mm = mmv
                        if s_i == NSUB - 1:
                            slot_done(slot, "mm", mmv)
                        em.wait("vector", "mm", mmv)
                        dst = xv2[xb][:, s_i, g * 256:(g + 1) * 256]
                        vv = em.op("vector", lambda v, d=dst, b=bank:
                                   v.tensor_tensor(d, p[b][:, 0:256], d,
                                                   OP.add), inc="v")
                        pa_pool.release(bank, "v", vv)
                    resid_s[s_i] = vv
                    em.op("vector", lambda v, xb=xb, s=s_i: v.reduce_sum(
                        lnb[:, s:s + 1], xv2[xb][:, s:s + 1, :], axis=AX))
                    em.wait("scalar", "v", vv)
                    scv0 = em.op("scalar", lambda sc, xb=xb, s=s_i:
                                 sc.activation(
                                     Pscr[:, s * D:(s + 1) * D],
                                     x_t[xb][:, s * D:(s + 1) * D],
                                     AF.Square), inc="sc")
                    em.wait("vector", "sc", scv0)
                    em.op("vector", lambda v, s=s_i: v.reduce_sum(
                        lnb[:, 4 + s:5 + s],
                        Pscr[:, s * D:(s + 1) * D].unsqueeze(1), axis=AX))
                resid_done = vv
            ST["outproj_mm"] = last_mm
            if ti == 0:
                em.op("vector", lambda v, xb=xb: v.reduce_sum(
                    lnb[:, 0:2], xv2[xb], axis=AX))
                em.wait("scalar", "v", resid_done)
                scv = em.op("scalar", lambda sc, xb=xb: sc.activation(
                    Pscr[:, 0:2048], x_t[xb][:], AF.Square), inc="sc")
                em.wait("vector", "sc", scv)
                em.op("vector", lambda v: v.reduce_sum(
                    lnb[:, 4:6],
                    Pscr[:, 0:2048].rearrange("q (s f) -> q s f", f=D),
                    axis=AX))
            em.op("vector", lambda v: v.tensor_scalar_mul(
                lnb[:, 8:10], lnb[:, 0:2], 1.0 / D))
            vpad()
            em.op("vector", lambda v: v.tensor_mul(
                lnb[:, 12:14], lnb[:, 8:10], lnb[:, 8:10]))
            vpad()
            em.op("vector", lambda v: v.scalar_tensor_tensor(
                lnb[:, 16:18], lnb[:, 4:6], 1.0 / D, lnb[:, 12:14],
                OP.mult, OP.subtract))
            vpad()
            vv = em.op("vector", lambda v: v.tensor_scalar_add(
                lnb[:, 20:22], lnb[:, 16:18], 1e-5), inc="v")
            em.wait("scalar", "v", vv)
            scv = em.op("scalar", lambda sc: sc.activation(
                lnb[:, 24:26], lnb[:, 20:22], AF.Sqrt), inc="sc")
            em.wait("vector", "sc", scv)
            em.op("vector", lambda v: v.reciprocal(lnb[:, 28:30], lnb[:, 24:26]))
            vpad()
            em.op("vector", lambda v: v.tensor_mul(
                lnb[:, 32:34], lnb[:, 28:30], lnb[:, 28:30]))
            vpad()
            em.op("vector", lambda v: v.tensor_mul(
                lnb[:, 36:38], lnb[:, 20:22], lnb[:, 32:34]))
            vpad()
            em.op("vector", lambda v: v.tensor_scalar(
                lnb[:, 40:42], lnb[:, 36:38], -0.5, 1.5, OP.mult, OP.add))
            vpad()
            em.op("vector", lambda v: v.tensor_mul(
                lnb[:, 44:46], lnb[:, 28:30], lnb[:, 40:42]))
            vpad()
            for s_i in range(NSUB):
                rb = lnb[:, 44 + s_i:45 + s_i].broadcast_to((128, D))
                fn = (lambda v, s=s_i, rb=rb, xb=xb: v.scalar_tensor_tensor(
                    ln1[:, s * D:(s + 1) * D], xv2[xb][:, s, :],
                    lnb[:, 8 + s:9 + s], rb, OP.subtract, OP.mult))
                vv = em.op("vector", fn, inc="v")
            ST["ln1_apply"][xb] = vv
            if ti + 2 < ntiles:
                em.wait("sync", "v", vv)
                for s_i in range(NSUB):
                    em.op("sync", lambda sy, s=s_i, t=ti, xb=xb: sy.dma_start(
                        out=x_t[xb][:, s * D:(s + 1) * D], in_=hx[t + 2, s]),
                        inc="x", k=16)
            em.wait("scalar", "v", vv)
            scv = em.op("scalar", lambda sc: sc.copy(hi_tm, ln1[:]), inc="sc")
            em.wait("vector", "sc", scv)
            spl = em.op("vector", lambda v: v.tensor_sub(
                lo_tm, ln1[:], hi_tm), inc="v")
            l1fv = 0
            for buf_i, (src_tm, dst_fm) in enumerate(
                    [(hi_tm, l1_hi), (lo_tm, l1_lo)]):
                for cc in range(2):
                    bank = pt_pool.acquire("tensor")
                    if buf_i == 0 and cc == 0:
                        em.wait("tensor", "sc", scv)
                    if buf_i == 1 and cc == 0:
                        em.wait("tensor", "v", spl)
                    for q in range(4):
                        c = cc * 4 + q
                        for s_i in range(NSUB):
                            fn = (lambda t, b=bank, q=q, s=s_i, c=c, src=src_tm:
                                  t.transpose(
                                      p[b][:].bitcast(BF16)[
                                          :, (q * 2 + s) * 128:(q * 2 + s + 1) * 128],
                                      src.rearrange("q (s f) -> q s f", f=D)[
                                          :, s, c * 128:(c + 1) * 128],
                                      idb_sb[:]))
                            mmv = em.op("tensor", fn,
                                        inc=("mm" if (q == 3 and s_i == 1)
                                             else None))
                    em.wait("vector", "mm", mmv)
                    dst = dst_fm.rearrange("q (c t) -> q c t", t=256)[
                        :, cc * 4:cc * 4 + 4, :]
                    src = p[bank][:].bitcast(BF16)[:, 0:1024] \
                        .rearrange("q (c t) -> q c t", t=256)
                    vv2 = em.op("vector", lambda v, d=dst, s=src:
                                v.tensor_copy(d, s), inc="v")
                    pt_pool.release(bank, "v", vv2)
                    l1fv = vv2
            ST["ln1tp_mm"] = mmv
            ST["l1f_done"][ti] = l1fv

        def emit_ffn_a(ti):
            em.ctx = f"ffna{ti}"
            conv = (ti == 0)
            h_gp = h_s = 0
            for gp in range(8):
                sl1 = slots[(2 * gp + 2) % 4]
                sl2 = slots[(2 * gp + 3) % 4]
                for sl, fsrc, zsrc in ((sl1, hw1[gp], zw1[gp]),
                                       (sl2, hw2[gp], zw2[gp])):
                    wv_ = slot_dma(sl, fsrc if conv else zsrc, dst_bf=True)
                    sl["lastwv"] = wv_
                    if conv:
                        scratch_write(sl, zsrc, wv_)
                for j in range(4):
                    oc = gp * 4 + j
                    bA = pa_pool.acquire("tensor")
                    if j == 0:
                        em.wait("tensor", sl1["sem"], sl1["lastwv"])
                        em.wait("tensor", sl2["sem"], sl2["lastwv"])
                        if gp == 0:
                            em.wait("tensor", "v", ST["l1f_done"][ti])
                            if ti > 0:
                                em.wait("tensor", "st", 16 * NSUB * ti)
                    for k in range(8):
                        for vi, rhs in ((0, l1_hi), (1, l1_lo)):
                            fn = (lambda t, b=bA, k=k, j=j, sl=sl1, r=rhs, vi=vi:
                                  t.matmul(p[b][:, 0:256],
                                           sl["bf"][:, k * 512 + j * 128:k * 512 + (j + 1) * 128],
                                           r[:, k * 256:(k + 1) * 256],
                                           start=(k == 0 and vi == 0),
                                           stop=(k == 7 and vi == 1)))
                            mmA = em.op("tensor", fn,
                                        inc=("mm" if (k == 7 and vi == 1) else None))
                    bB = pa_pool.acquire("tensor")
                    for k in range(8):
                        for vi, rhs in ((0, l1_hi), (1, l1_lo)):
                            fn = (lambda t, b=bB, k=k, j=j, sl=sl2, r=rhs, vi=vi:
                                  t.matmul(p[b][:, 0:256],
                                           sl["bf"][:, k * 512 + j * 128:k * 512 + (j + 1) * 128],
                                           r[:, k * 256:(k + 1) * 256],
                                           start=(k == 0 and vi == 0),
                                           stop=(k == 7 and vi == 1)))
                            mmB = em.op("tensor", fn,
                                        inc=("mm" if (k == 7 and vi == 1) else None))
                    if j == 3:
                        slot_done(sl1, "mm", mmB)
                        slot_done(sl2, "mm", mmB)
                    fslot = fm_pool.acquire("scalar")
                    em.wait("scalar", "mm", mmA)
                    sA = fmst[:, fslot * 256:(fslot + 1) * 256]
                    scv = em.op("scalar", lambda sc, o=sA, b=bA, oc=oc:
                                sc.activation(o, p[b][:, 0:256], AF.Silu,
                                              scale=s1_sb[:, oc:oc + 1]),
                                inc="sc")
                    pa_pool.release(bA, "sc", scv)
                    fslot2 = fm_pool.acquire("vector")
                    em.wait("vector", "mm", mmB)
                    em.wait("vector", "sc", scv)
                    hout = fmst[:, fslot2 * 256:(fslot2 + 1) * 256]
                    vv = em.op("vector", lambda v, o=hout, b=bB, oc=oc, sA=sA:
                               v.scalar_tensor_tensor(
                                   o, p[b][:, 0:256], s2_sb[:, oc:oc + 1], sA,
                                   OP.mult, OP.mult), inc="v")
                    pa_pool.release(bB, "v", vv)
                    fm_pool.release(fslot, "v", vv)
                    em.wait("scalar", "v", vv)
                    hidst = h_hi[:, oc * 256:(oc + 1) * 256]
                    scv = em.op("scalar", lambda sc, d=hidst, s=hout:
                                sc.copy(d, s), inc="sc")
                    em.wait("vector", "sc", scv)
                    lodst = h_lo[:, oc * 256:(oc + 1) * 256]
                    vv = em.op("vector", lambda v, d=lodst, s=hout, hi=hidst:
                               v.tensor_sub(d, s, hi), inc="v")
                    fm_pool.release(fslot2, "v", vv)
                    h_gp, h_s = vv, scv
            ST["hsplit"] = (h_gp, h_s)

        def emit_ffn_b(ti):
            em.ctx = f"ffnb{ti}"
            conv = (ti == 0)
            h_gp, h_s = ST["hsplit"]

            def h_stage(fsrc_arr, zsrc_arr, scale_sb, in_hi, in_lo,
                        o_hi, o_lo, gate_gp, gate_s, mid_ti=None):
                last_gp = last_s = 0
                pend = []
                for oc in range(33):
                    if mid_ti is not None and oc == 1:
                        emit_midA(mid_ti, 0)
                        em.ctx = f"ffnb{ti}"
                    if mid_ti is not None and oc == 8:
                        emit_midA(mid_ti, 1)
                        em.ctx = f"ffnb{ti}"
                    if mid_ti is not None and oc == 12:
                        emit_midB(mid_ti)
                        em.ctx = f"ffnb{ti}"
                    if mid_ti is not None and oc == 13:
                        emit_midC(mid_ti, 0)
                        em.ctx = f"ffnb{ti}"
                    if mid_ti is not None and oc == 20:
                        emit_midC(mid_ti, 1)
                        em.ctx = f"ffnb{ti}"
                    if oc < 32:
                        sl = slots[oc % 4]
                        wv_ = slot_dma(sl, fsrc_arr[oc] if conv else zsrc_arr[oc],
                                       dst_bf=True)
                        if conv:
                            scratch_write(sl, zsrc_arr[oc], wv_)
                        bank = pa_pool.acquire("tensor")
                        em.wait("tensor", sl["sem"], wv_)
                        if oc == 0:
                            em.wait("tensor", "v", gate_gp)
                            em.wait("tensor", "sc", gate_s)
                        for k in range(32):
                            for vi, rhs in ((0, in_hi), (1, in_lo)):
                                fn = (lambda t, b=bank, k=k, sl=sl, r=rhs, vi=vi:
                                      t.matmul(p[b][:, 0:256],
                                               sl["bf"][:, k * 128:(k + 1) * 128],
                                               r[:, k * 256:(k + 1) * 256],
                                               start=(k == 0 and vi == 0),
                                               stop=(k == 31 and vi == 1)))
                                mmv = em.op("tensor", fn,
                                            inc=("mm" if (k == 31 and vi == 1)
                                                 else None))
                        slot_done(sl, "mm", mmv)
                        fslot = fm_pool.acquire("scalar")
                        em.wait("scalar", "mm", mmv)
                        gbuf = fmst[:, fslot * 256:(fslot + 1) * 256]
                        scv = em.op("scalar", lambda sc, o=gbuf, b=bank, oc=oc,
                                    ssb=scale_sb: sc.activation(
                                        o, p[b][:, 0:256], AF.Gelu,
                                        scale=ssb[:, oc:oc + 1]), inc="sc")
                        pa_pool.release(bank, "sc", scv)
                        hidst = o_hi[:, oc * 256:(oc + 1) * 256]
                        scv = em.op("scalar", lambda sc, d=hidst, s=gbuf:
                                    sc.copy(d, s), inc="sc")
                        lodst = o_lo[:, oc * 256:(oc + 1) * 256]
                        pend.append((scv, lodst, gbuf, hidst, fslot))
                        last_s = scv
                    while len(pend) > 0:
                        scv_, lodst_, gbuf_, hidst_, fs_ = pend.pop(0)
                        em.wait("vector", "sc", scv_)
                        gv = em.op("vector", lambda v, d=lodst_, s=gbuf_,
                                   hi=hidst_: v.tensor_sub(d, s, hi), inc="v")
                        fm_pool.release(fs_, "v", gv)
                        last_gp = gv
                return last_gp, last_s

            g1gp, g1s = h_stage(hh1, zh1, h1s_sb, h_hi, h_lo, g1_hi, g1_lo,
                                h_gp, h_s,
                                mid_ti=(ti + 1 if ti + 1 < ntiles else None))
            g2gp, g2s = h_stage(hh2, zh2, h2s_sb, g1_hi, g1_lo, h_hi, h_lo,
                                g1gp, g1s)

            if ti + 1 < ntiles:
                em.ctx = f"tailsplit{ti + 1}"
                em.wait("scalar", "v", ST["ctx_done"][ti + 1])
                hv_ = em.op("scalar", lambda sc: sc.copy(hi_tm, ctx_tm),
                            inc="sc")
                em.wait("vector", "sc", hv_)
                lv_ = em.op("vector", lambda v: v.tensor_sub(
                    lo_tm, ctx_tm, hi_tm), inc="v")
                ST["ctx_split"] = (hv_, lv_)
                ctxfm_done = 0
                for buf_i, (src_tm, dst_fm) in enumerate(
                        [(hi_tm, xf_hi), (lo_tm, xf_lo)]):
                    for cc in range(2):
                        bank = pt_pool.acquire("tensor")
                        if buf_i == 0 and cc == 0:
                            em.wait("tensor", "sc", hv_)
                        if buf_i == 1 and cc == 0:
                            em.wait("tensor", "v", lv_)
                        for q in range(4):
                            c = cc * 4 + q
                            for s_i in range(NSUB):
                                fn = (lambda t, b=bank, q=q, s=s_i, c=c,
                                      src=src_tm: t.transpose(
                                          p[b][:].bitcast(BF16)[
                                              :, (q * 2 + s) * 128:(q * 2 + s + 1) * 128],
                                          src.rearrange("q (s f) -> q s f", f=D)[
                                              :, s, c * 128:(c + 1) * 128],
                                          idb_sb[:]))
                                mmv = em.op("tensor", fn,
                                            inc=("mm" if (q == 3 and s_i == 1)
                                                 else None))
                        em.wait("vector", "mm", mmv)
                        dst = dst_fm.rearrange("q (c t) -> q c t", t=256)[
                            :, cc * 4:cc * 4 + 4, :]
                        src = p[bank][:].bitcast(BF16)[:, 0:1024] \
                            .rearrange("q (c t) -> q c t", t=256)
                        vv_ = em.op("vector", lambda v, d=dst, s=src:
                                    v.tensor_copy(d, s), inc="v")
                        pt_pool.release(bank, "v", vv_)
                        ctxfm_done = vv_
                ST["ctxfm_done"] = ctxfm_done
                em.ctx = f"ffnb{ti}"
            for it in pt_pool.items:
                fa = pt_pool.free_at[it]
                if fa:
                    em.wait("tensor", *fa)
            em.wait("tensor", "v", g2gp)
            em.wait("tensor", "sc", g2s)
            if ST.get("w3_evac"):
                em.wait("tensor", "v", ST["w3_evac"])
            for gw in range(8):
                sl = slots[gw % 2]
                wv_ = slot_dma(sl, hw3[gw] if conv else zw3[gw], dst_bf=True,
                               eng=("gpsimd" if conv else "sync"))
                if conv:
                    scratch_write(sl, zw3[gw], wv_)
                em.wait("tensor", sl["sem"], wv_)
                for kk in range(4):
                    k = gw * 4 + kk
                    for src in (h_hi, h_lo):
                        for s_i in range(NSUB):
                            for nh in range(2):
                                fn = (lambda t, k=k, kk=kk, s=s_i, nh=nh,
                                      sl=sl, src=src: t.matmul(
                                          p[2 + 2 * s + nh][:],
                                          src[:, k * 256 + s * 128:k * 256 + (s + 1) * 128],
                                          sl["bf"][:, kk * 1024 + nh * 512:kk * 1024 + (nh + 1) * 512],
                                          start=(k == 0 and src is h_hi),
                                          stop=(k == 31 and src is h_lo)))
                                mmv = em.op("tensor", fn,
                                            inc=("mm" if (kk == 3 and
                                                          src is h_lo and
                                                          s_i == 1 and nh == 1)
                                                 else None))
                slot_done(sl, "mm", em.val("mm"))
            w3mm = em.val("mm")
            em.wait("vector", "mm", w3mm)
            for s_i in range(NSUB):
                for nh in range(2):
                    dst = y2[:, s_i * D + nh * 512:s_i * D + (nh + 1) * 512]
                    lsrc = ln1[:, s_i * D + nh * 512:s_i * D + (nh + 1) * 512]
                    em.op("vector", lambda v, d=dst, s=s_i, nh=nh:
                          v.tensor_tensor(d, p[2 + 2 * s + nh][:],
                                          s3_sb[:, nh * 512:(nh + 1) * 512],
                                          OP.mult))
                    vv = em.op("vector", lambda v, d=dst, ls=lsrc:
                               v.tensor_tensor(d, d, ls, OP.add), inc="v")
            w3_evac = vv
            ST["w3_evac"] = w3_evac
            for b_ in (2, 3, 4, 5):
                pa_pool.free_at[b_] = ("v", w3_evac)
            y2v = y2.rearrange("q (s f) -> q s f", f=D)
            em.op("vector", lambda v: v.reduce_sum(lnb[:, 0:2], y2v, axis=AX))
            em.wait("scalar", "v", w3_evac)
            scv = em.op("scalar", lambda sc: sc.activation(
                sq2, y2, AF.Square), inc="sc")
            em.wait("vector", "sc", scv)
            em.op("vector", lambda v: v.reduce_sum(
                lnb[:, 4:6], sq2.rearrange("q (s f) -> q s f", f=D), axis=AX))
            em.op("vector", lambda v: v.tensor_scalar_mul(
                lnb[:, 8:10], lnb[:, 0:2], 1.0 / D))
            vpad()
            em.op("vector", lambda v: v.tensor_mul(
                lnb[:, 12:14], lnb[:, 8:10], lnb[:, 8:10]))
            vpad()
            em.op("vector", lambda v: v.scalar_tensor_tensor(
                lnb[:, 16:18], lnb[:, 4:6], 1.0 / D, lnb[:, 12:14],
                OP.mult, OP.subtract))
            vpad()
            vv = em.op("vector", lambda v: v.tensor_scalar_add(
                lnb[:, 20:22], lnb[:, 16:18], 1e-5), inc="v")
            em.wait("scalar", "v", vv)
            scv = em.op("scalar", lambda sc: sc.activation(
                lnb[:, 24:26], lnb[:, 20:22], AF.Sqrt), inc="sc")
            em.wait("vector", "sc", scv)
            em.op("vector", lambda v: v.reciprocal(lnb[:, 28:30], lnb[:, 24:26]))
            vpad()
            em.op("vector", lambda v: v.tensor_mul(
                lnb[:, 32:34], lnb[:, 28:30], lnb[:, 28:30]))
            vpad()
            em.op("vector", lambda v: v.tensor_mul(
                lnb[:, 36:38], lnb[:, 20:22], lnb[:, 32:34]))
            vpad()
            em.op("vector", lambda v: v.tensor_scalar(
                lnb[:, 40:42], lnb[:, 36:38], -0.5, 1.5, OP.mult, OP.add))
            vpad()
            em.op("vector", lambda v: v.tensor_mul(
                lnb[:, 44:46], lnb[:, 28:30], lnb[:, 40:42]))
            vpad()
            for s_i in range(NSUB):
                rb = lnb[:, 44 + s_i:45 + s_i].broadcast_to((128, D))
                fn = (lambda v, s=s_i, rb=rb: v.scalar_tensor_tensor(
                    out_tm[:, s * D:(s + 1) * D], y2v[:, s, :],
                    lnb[:, 8 + s:9 + s], rb, OP.subtract, OP.mult))
                vv = em.op("vector", fn, inc="v")
            ST["out_done"] = vv
            em.wait("scalar", "v", vv)
            for s_i in range(NSUB):
                em.op("scalar", lambda sc, s=s_i, t=ti: sc.dma_start(
                    out=ho[t, s], in_=out_tm[:, s * D:(s + 1) * D]),
                    inc="st", k=16)

        emit_head(0)
        emit_mid(0)
        emit_tail(0)
        for i in range(ntiles):
            if i + 1 < ntiles:
                emit_head(i + 1)
            emit_ffn_a(i)
            emit_ffn_b(i)
            if i + 1 < ntiles:
                emit_tail(i + 1)
        em.wait("scalar", "st", em.val("st"))

        def replay(eng_name):
            def run(e):
                for item in em.ops[eng_name]:
                    if item[0] == "w":
                        e.wait_ge(sems[item[1]], item[2])
                    else:
                        _, fn, incn, k = item[:4]
                        instr = fn(e)
                        if incn is not None:
                            instr.then_inc(sems[incn], k)
            return run

        blk.sync(replay("sync"))
        blk.gpsimd(replay("gpsimd"))
        blk.tensor(replay("tensor"))
        blk.vector(replay("vector"))
        blk.scalar(replay("scalar"))

    globals()['LAST_EM'] = em
    return nc


def _prep_weights(inputs):
    f32 = np.float32

    def c(a):
        return np.ascontiguousarray(a, dtype=f32)

    Wq = np.asarray(inputs["L0_attn_W_qkv"], dtype=f32)
    Wo = np.asarray(inputs["L0_attn_W_out"], dtype=f32)
    w1T = np.asarray(inputs["L0_ffn_w1_w"], dtype=f32).T
    w2T = np.asarray(inputs["L0_ffn_w2_w"], dtype=f32).T
    h1T = np.asarray(inputs["L0_ffn_h1_w"], dtype=f32).T
    h2T = np.asarray(inputs["L0_ffn_h2_w"], dtype=f32).T
    w3T = np.asarray(inputs["L0_ffn_w3_w"], dtype=f32).T
    d = {
        "wq": c(Wq.reshape(8, 128, 12, 256).transpose(2, 1, 0, 3)
                .reshape(12, 128, 2048)),
        "wo": c(Wo.reshape(8, 128, 4, 256).transpose(2, 1, 0, 3)
                .reshape(4, 128, 2048)),
        "w1": c(w1T.reshape(8, 128, 8, 512).transpose(2, 1, 0, 3)
                .reshape(8, 128, 4096)),
        "w2": c(w2T.reshape(8, 128, 8, 512).transpose(2, 1, 0, 3)
                .reshape(8, 128, 4096)),
        "h1": c(h1T.reshape(32, 128, 32, 128).transpose(2, 1, 0, 3)
                .reshape(32, 128, 4096)),
        "h2": c(h2T.reshape(32, 128, 32, 128).transpose(2, 1, 0, 3)
                .reshape(32, 128, 4096)),
        "w3": c(w3T.reshape(8, 4, 128, 1024).transpose(0, 2, 1, 3)
                .reshape(8, 128, 4096)),
        "bqbc": c(np.broadcast_to(
            np.asarray(inputs["L0_attn_b_qkv"], dtype=f32), (128, QKVD))),
        "bobc": c(np.broadcast_to(
            np.asarray(inputs["L0_attn_b_out"], dtype=f32), (128, D))),
        "s1": c(np.asarray(inputs["L0_ffn_w1_s"], dtype=f32)
                .reshape(32, 128).T),
        "s2": c(np.asarray(inputs["L0_ffn_w2_s"], dtype=f32)
                .reshape(32, 128).T),
        "h1s": c(np.asarray(inputs["L0_ffn_h1_s"], dtype=f32)
                 .reshape(32, 128).T),
        "h2s": c(np.asarray(inputs["L0_ffn_h2_s"], dtype=f32)
                 .reshape(32, 128).T),
        "s3bc": c(np.broadcast_to(
            np.asarray(inputs["L0_ffn_w3_s"], dtype=f32), (128, D))),
        "idf": np.eye(128, dtype=f32),
        "idb": np.eye(128, dtype=ml_dtypes.bfloat16),
    }
    return d


def kernel(**inputs):
    x = np.ascontiguousarray(
        np.asarray(inputs["L0_attn_x"], dtype=np.float32).reshape(-1, D))
    ntok = x.shape[0]
    per = ntok // NCORES
    wd = _prep_weights(inputs)
    nc = build_nc(NTILES)
    in_maps = []
    for ci in range(NCORES):
        m = dict(wd)
        m["x"] = np.ascontiguousarray(
            x[ci * per:(ci + 1) * per].reshape(NTILES, NSUB, 128, D))
        in_maps.append(m)
    res = run_bass_kernel_spmd(nc, in_maps, core_ids=list(range(NCORES)))
    out = np.concatenate(
        [np.asarray(r["out"]).reshape(per, D) for r in res.results], axis=0)
    return out.reshape(np.asarray(inputs["L0_attn_x"]).shape)
